# version 09
# speedup vs baseline: 1.2501x; 1.0691x over previous
import math
from contextlib import ExitStack

import numpy as np
import ml_dtypes

import concourse.bass as bass
import concourse.mybir as mybir
from concourse.bass_utils import run_bass_kernel_spmd

F32 = mybir.dt.float32
BF16 = mybir.dt.bfloat16
AF = mybir.ActivationFunctionType
OP = mybir.AluOpType
AX = mybir.AxisListType

NCORES = 8
S = 4096
D = 1024
NCH = 8
CH = 512
NCT = 8
BF = ml_dtypes.bfloat16

L0_H = 16
L0_F = 16
L0_E = 64
RD = 4.0
RRD = 2.0
SQS = 1.0 / (math.sqrt(2.0) * RD)
A_SC = 1.0 / math.sqrt(32.0)
A_B = math.sqrt(2.0) / 2.0

F_H = 8
F_E = 64
F_INNER = 512
F_WIN = 512
F_HID = 2730
F_HIDP = 2816
F_NCT = 22
QK_SCALE = 8.0

E_ROUTED = 8
MOE_I = 2048

NEG = -1.0e9



def _rotary_tables():
    n, dh, window = S, F_E, F_WIN
    half = np.arange(0, dh, 2, dtype=np.float64)
    inv_freq = 1.0 / 10000.0 ** (half / dh)
    t = np.arange(n, dtype=np.float64)
    freqs = t[:, None] * inv_freq[None, :]
    freqs = np.concatenate([freqs, freqs], axis=-1)
    base = (half + 0.4 * dh) / (1.4 * dh)
    power = (t - n // 2) / float(window // 2)
    scale = base[None, :] ** power[:, None]
    scale = np.concatenate([scale, scale], axis=-1)
    cos, sin = np.cos(freqs), np.sin(freqs)
    sgn = np.ones((1, dh))
    sgn[:, :dh // 2] = -1.0
    aq = (cos * scale).astype(np.float32)
    bq = (sgn * sin * scale).astype(np.float32)
    ak = (cos / scale).astype(np.float32)
    bk = (sgn * sin / scale).astype(np.float32)
    return aq, bq, ak, bk


def _consts():
    c = {}
    c["c_ident"] = np.eye(128, dtype=np.float32)
    c["c_ones_row"] = np.ones((1, 512), dtype=np.float32)
    c["c_ones_col"] = np.ones((128, 1), dtype=np.float32)
    ob = np.zeros((128, 2), dtype=np.float32)
    ob[:64, 0] = 1.0
    ob[64:, 1] = 1.0
    c["c_onesblk"] = ob
    b2 = np.zeros((2, 128), dtype=np.float32)
    b2[0, :64] = 1.0
    b2[1, 64:] = 1.0
    c["c_blk2"] = b2
    es = np.zeros((8, 8 * 128), dtype=np.float32)
    for e in range(8):
        es[e, 128 * e:128 * (e + 1)] = 1.0
    c["c_esel"] = es
    m = np.ones((128, 512), dtype=np.float32)
    p = np.arange(128)[:, None]
    cc = np.arange(128)[None, :]
    m[:, :128] = (p <= cc).astype(np.float32)
    c["c_triu_mul"] = m
    md = np.where(p < cc, 0.0, 0.0) + np.where(cc > p, NEG, 0.0)
    c["c_tri_add"] = md.astype(np.float32)
    ii = np.arange(512)[:, None]
    jj = np.arange(1024)[None, :]
    ml = np.where((jj > ii + 512) | (jj < ii), NEG, 0.0).astype(np.float32)
    c["c_lmask"] = ml
    c["c_mlo"] = np.where(cc < p, NEG, 0.0).astype(np.float32)
    c["c_mhi"] = np.where(cc > p, NEG, 0.0).astype(np.float32)
    aq, bq, ak, bk = _rotary_tables()
    c["c_rot_aq"] = aq
    c["c_rot_bq"] = bq
    c["c_rot_ak"] = ak
    c["c_rot_bk"] = bk
    return c



def _pad_cols(a, n):
    out = np.zeros((a.shape[0], n), dtype=a.dtype)
    out[:, :a.shape[1]] = a
    return out


def _pad_rows(a, n):
    out = np.zeros((n, a.shape[1]), dtype=a.dtype)
    out[:a.shape[0]] = a
    return out


def stage_weights(inp):
    w = {}
    wq = np.asarray(inp["L0_attn_Wq"])
    wk = np.asarray(inp["L0_attn_Wk"])
    w["l0_qw"] = np.ascontiguousarray(wq.T)
    w["l0_kw"] = np.ascontiguousarray(wk.T)
    q32 = np.zeros((1024, 512), dtype=np.float32)
    k32 = np.zeros((1024, 512), dtype=np.float32)
    for h in range(16):
        q32[:, 32 * h:32 * h + 16] = wq[16 * h:16 * h + 16, :].T
        k32[:, 32 * h:32 * h + 16] = wk[16 * h:16 * h + 16, :].T
    w["l0_q32w"] = q32
    w["l0_k32w"] = k32
    w["l0_vw"] = np.ascontiguousarray(np.asarray(inp["L0_attn_Wv"]).T)
    w["l0_ow"] = np.ascontiguousarray(np.asarray(inp["L0_attn_Wo"]).T)
    w["l0_g"] = np.ascontiguousarray(np.tile(np.asarray(inp["L0_attn_g_weight"]), 2)[:, None])
    wqkv = np.asarray(inp["L0_ffn_wqkv"])
    wo = np.asarray(inp["L0_ffn_wo"])
    w1 = np.asarray(inp["L0_ffn_w1"])
    b1 = np.asarray(inp["L0_ffn_b1"])
    w2 = np.asarray(inp["L0_ffn_w2"])
    for l in range(2):
        w[f"f{l}_qkv"] = np.ascontiguousarray(wqkv[l])
        w[f"f{l}_wo"] = np.ascontiguousarray(wo[l])
        w[f"f{l}_w1a"] = _pad_cols(np.ascontiguousarray(w1[l][:, :F_HID]), F_HIDP)
        w[f"f{l}_w1g"] = _pad_cols(np.ascontiguousarray(w1[l][:, F_HID:]), F_HIDP)
        w[f"f{l}_b1a"] = _pad_cols(b1[l][None, :F_HID], F_HIDP)
        w[f"f{l}_b1g"] = _pad_cols(b1[l][None, F_HID:], F_HIDP)
        w[f"f{l}_w2"] = _pad_rows(np.ascontiguousarray(w2[l]), F_HIDP)
        w[f"f{l}_b2"] = np.ascontiguousarray(np.asarray(inp["L0_ffn_b2"])[l][None, :])
        w[f"f{l}_lng"] = np.ascontiguousarray(
            np.asarray(inp["L0_ffn_ln_g"])[l].reshape(8, 128).T)
        w[f"f{l}_lnb"] = np.ascontiguousarray(
            np.asarray(inp["L0_ffn_ln_b"])[l].reshape(8, 128).T)
        w[f"f{l}_qs"] = np.ascontiguousarray(np.tile(np.asarray(inp["L0_ffn_q_scale"])[l], 8)[None, :])
        w[f"f{l}_ks"] = np.ascontiguousarray(np.tile(np.asarray(inp["L0_ffn_k_scale"])[l], 8)[None, :])
    w["a1_qkv"] = np.ascontiguousarray(np.asarray(inp["L1_attn_qkv_w"]).T)
    w["a1_ow"] = np.ascontiguousarray(np.asarray(inp["L1_attn_out_w"]).T)
    w["a1_ob"] = np.ascontiguousarray(np.asarray(inp["L1_attn_out_b"])[None, :])
    def _gu_interleave(g, u):
        out = np.empty((g.shape[0], 4096), dtype=np.float32)
        for ot in range(16):
            out[:, 256 * ot:256 * ot + 128] = g[:, 128 * ot:128 * (ot + 1)]
            out[:, 256 * ot + 128:256 * (ot + 1)] = u[:, 128 * ot:128 * (ot + 1)]
        return out
    gs = np.asarray(inp["L1_ffn_Wg_s"]).T
    us = np.asarray(inp["L1_ffn_Wu_s"]).T
    w["m_gus"] = _gu_interleave(gs, us)
    w["m_ds"] = np.ascontiguousarray(np.asarray(inp["L1_ffn_Wd_s"]).T)
    gg = np.asarray(inp["L1_ffn_Wg"]).transpose(0, 2, 1)
    uu = np.asarray(inp["L1_ffn_Wu"]).transpose(0, 2, 1)
    w["m_gu"] = np.stack([_gu_interleave(gg[e], uu[e]) for e in range(8)])
    w["m_d"] = np.ascontiguousarray(np.asarray(inp["L1_ffn_Wd"]).transpose(0, 2, 1))
    w["m_r"] = np.ascontiguousarray(np.asarray(inp["L1_ffn_Wr"]).T)
    w["m_rb"] = np.ascontiguousarray(np.asarray(inp["L1_ffn_rb"])[None, :])
    return w



ENGS = ("sync", "tensor", "vector", "scalar", "gpsimd")


class Ctx:

    def __init__(self, nc):
        self.nc = nc
        self.stream = {e: [] for e in ENGS}
        self.cnt = {}
        self.mark = {}
        self.waited = {}
        self.last_real = dict.fromkeys(ENGS, 0)
        self.real_cnt = {}

    def op(self, eng, fn, deps=(), dma_n=0, sem=None, noinst=False):
        deps = [d for d in deps if d is not None]
        name = sem or eng
        if eng in ("vector", "scalar") and self.last_real.get(eng, 0) > 0:
            deps.append((eng, self.last_real[eng]))
        if eng == "tensor":
            deps = [d for d in deps if d[0] != "tensor"]
        inc = 16 * dma_n if dma_n else 1
        self.cnt[name] = self.cnt.get(name, 0) + inc
        token = (name, self.cnt[name])
        waits = []
        for pn, val in deps:
            key = (eng, pn)
            if self.mark.get(key, 0) >= val:
                continue
            self.mark[key] = val
            waits.append((pn, val))
            self.waited.setdefault(pn, set()).add(val)
        self.stream[eng].append((name, token[1], waits, fn, inc))
        if not noinst:
            self.real_cnt[name] = token[1]
            if not dma_n:
                self.last_real[eng] = token[1]
        return token

    def dma(self, queue, sem, pairs, deps=()):
        def fn(e, s):
            for dst, src in pairs:
                _inc(e.dma_start(out=dst, in_=src), s, 16)
        return self.op(queue, fn, deps, dma_n=len(pairs), sem=sem)

    def mm(self, mms, deps=()):
        def fn(e, s):
            last = None
            for m in mms:
                last = e.matmul(m["out"], m["lhsT"], m["rhs"], start=m.get("start", True),
                                stop=m.get("stop", True), tile_position=m.get("tp"))
            _inc(last, s, 1)
        return self.op("tensor", fn, deps)

    def tr(self, out, in_, ident, deps=()):
        def fn(e, s):
            _inc(e.transpose(out, in_, ident), s, 1)
        return self.op("tensor", fn, deps)

    def v(self, fn, deps=()):
        return self.op("vector", lambda e, s: _inc(fn(e), s, 1), deps)

    def s(self, fn, deps=()):
        return self.op("scalar", lambda e, s: _inc(fn(e), s, 1), deps)

    def t(self, fn, deps=()):
        return self.op("tensor", lambda e, s: _inc(fn(e), s, 1), deps)

    def g(self, fn, deps=()):
        return self.op("gpsimd", lambda e, s: _inc(fn(e), s, 1), deps)

    def emit(self, block, sems):
        remap = {}
        for name, waited in self.waited.items():
            remap[name] = waited
        acc = {}
        val_of = {n: {} for n in self.cnt}

        def build_valmap(name):
            items = []
            for e in ENGS:
                for (n, logical, waits, fn, inc) in self.stream[e]:
                    if n == name:
                        items.append((logical, inc))
            items.sort()
            a = 0
            for logical, inc in items:
                if logical in self.waited.get(name, ()):
                    a += inc
                val_of[name][logical] = a
        for name in self.cnt:
            build_valmap(name)

        def mk_body(eng):
            items = self.stream[eng]

            def body(e):
                for name, logical, waits, fn, inc in items:
                    for pn, val in waits:
                        e.wait_ge(sems[pn], val_of[pn][val])
                    fn(e, sems[name] if logical in self.waited.get(name, ()) else None)
            return body
        for eng in ENGS:
            if self.stream[eng]:
                getattr(block, eng)(mk_body(eng))


def _inc(ins, sem, n):
    if sem is not None:
        ins.then_inc(sem, n)
    return ins



STAGE_WEIGHTS = {
    "s1": ["l0_qw", "l0_kw", "l0_q32w", "l0_k32w", "l0_vw", "l0_ow", "l0_g"],
    "s2": sum([[f"f{l}_qkv", f"f{l}_wo", f"f{l}_w1a", f"f{l}_w1g", f"f{l}_b1a",
                f"f{l}_b1g", f"f{l}_w2", f"f{l}_b2", f"f{l}_lng", f"f{l}_lnb",
                f"f{l}_qs", f"f{l}_ks"] for l in range(2)], []),
    "s3": ["a1_qkv", "a1_ow", "a1_ob"],
    "s4": ["m_gus", "m_ds", "m_gu", "m_d", "m_r", "m_rb"],
}
STAGE_CONSTS = {
    "s1": ["c_ident", "c_ones_row", "c_ones_col", "c_onesblk", "c_blk2", "c_triu_mul"],
    "s2": ["c_ident", "c_ones_row", "c_ones_col", "c_mlo", "c_mhi",
           "c_rot_aq", "c_rot_bq", "c_rot_ak", "c_rot_bk"],
    "s3": ["c_ident", "c_ones_row", "c_ones_col", "c_tri_add"],
    "s4": ["c_ident", "c_ones_row", "c_ones_col", "c_esel"],
}

ALL_STAGES = ("s1", "s2", "s3", "s4")


class _SemDict(dict):
    def __init__(self, nc, root):
        super().__init__()
        self.nc, self.root = nc, root

    def __missing__(self, name):
        h = self.root.enter_context(self.nc.semaphore(f"sm_{name}"))
        self[name] = h
        return h



class Prog:
    def __init__(self, stages):
        self.stages = tuple(stages)
        nc = bass.Bass("TRN2", target_bir_lowering=False)
        self.nc = nc
        self.ctx = Ctx(nc)
        self.dram = {}
        keys = set()
        for st in self.stages:
            keys.update(STAGE_WEIGHTS[st])
            keys.update(STAGE_CONSTS[st])
        self.input_keys = sorted(keys)
        wshapes = _weight_shapes()
        for k in self.input_keys:
            self.dram[k] = nc.dram_tensor(k, list(wshapes[k]), F32, kind="ExternalInput").ap()
        self.dram["xin"] = nc.dram_tensor("xin", [D, S], F32, kind="ExternalInput").ap()
        self.dram["xout"] = nc.dram_tensor("xout", [D, S], F32, kind="ExternalOutput").ap()
        self.res_aps = []
        chain = [self.dram["xin"]]
        for i in range(len(self.stages) - 1):
            chain.append(nc.dram_tensor(f"xmid{i}", [D, S], F32, kind="Internal").ap())
        chain.append(self.dram["xout"])
        self.chain = chain
        self.store_toks = []

    def barrier(self):
        ctx = self.ctx
        toks = [(n, c) for n, c in ctx.real_cnt.items() if c > 0]
        for e in ENGS:
            ctx.op(e, lambda eng, sem: None, deps=toks, noinst=True)

    def build(self):
        nc, ctx = self.nc, self.ctx
        with ExitStack() as root:
            sems = _SemDict(nc, root)
            cs = {}
            cs["ident"] = root.enter_context(nc.sbuf_tensor("cs_ident", [128, 128], BF16))
            cs["ones_row_f"] = root.enter_context(nc.sbuf_tensor("cs_or_f", [1, 512], F32))
            cs["ones_row_b"] = root.enter_context(nc.sbuf_tensor("cs_or_b", [1, 512], BF16))
            cs["ones_col_f"] = root.enter_context(nc.sbuf_tensor("cs_oc_f", [128, 1], F32))
            cs["ones_col_b"] = root.enter_context(nc.sbuf_tensor("cs_oc_b", [128, 1], BF16))
            self.cs = cs
            dd = self.dram
            cast_loads = [(cs["ident"][:], dd["c_ident"]),
                          (cs["ones_row_b"][:], dd["c_ones_row"]),
                          (cs["ones_col_b"][:], dd["c_ones_col"])]
            f32_loads = [(cs["ones_row_f"][:], dd["c_ones_row"]),
                         (cs["ones_col_f"][:], dd["c_ones_col"])]
            if "s1" in self.stages:
                cs["onesblk"] = root.enter_context(nc.sbuf_tensor("cs_ob", [128, 2], F32))
                cs["blk2"] = root.enter_context(nc.sbuf_tensor("cs_b2", [2, 128], F32))
                cs["triu"] = root.enter_context(nc.sbuf_tensor("cs_triu", [128, 512], BF16))
                f32_loads += [(cs["onesblk"][:], dd["c_onesblk"]), (cs["blk2"][:], dd["c_blk2"])]
                cast_loads += [(cs["triu"][:], dd["c_triu_mul"])]
            if "s2" in self.stages:
                cs["mlo"] = root.enter_context(nc.sbuf_tensor("cs_mlo", [128, 128], F32))
                cs["mhi"] = root.enter_context(nc.sbuf_tensor("cs_mhi", [128, 128], F32))
                f32_loads += [(cs["mlo"][:], dd["c_mlo"]), (cs["mhi"][:], dd["c_mhi"])]
            if "s3" in self.stages:
                cs["tri_add"] = root.enter_context(nc.sbuf_tensor("cs_tri", [128, 128], F32))
                f32_loads += [(cs["tri_add"][:], dd["c_tri_add"])]
            if "s4" in self.stages:
                cs["esel"] = root.enter_context(nc.sbuf_tensor("cs_esel", [8, 1024], BF16))
                cast_loads += [(cs["esel"][:], dd["c_esel"])]
            cs["konst"] = root.enter_context(nc.sbuf_tensor("cs_k", [128, 4], F32))
            self.eps_col = cs["konst"][:, 0:1]
            self.ab_col = cs["konst"][:, 1:2]
            ctx.v(lambda e: e.memset(cs["konst"][:, 0:1], 1e-5))
            ctx.v(lambda e: e.memset(cs["konst"][:, 1:2], A_B))
            self.tok_consts_g = ctx.dma("gpsimd", "cg", cast_loads)
            self.tok_consts_s = ctx.dma("sync", "cs", f32_loads)
            self.barrier()

            for i, st in enumerate(self.stages):
                fn = {"s1": plan_s1, "s2": globals().get("plan_s2"),
                      "s3": globals().get("plan_s3"), "s4": globals().get("plan_s4")}[st]
                fn(self, self.chain[i], self.chain[i + 1])
                self.barrier()
            ctx.op("vector", lambda e, sem: None, deps=self.store_toks, noinst=True)

            for name in list(ctx.cnt):
                sems[name]
            block = root.enter_context(nc.Block())
            ctx.emit(block, sems)
        return nc


def _weight_shapes():
    sh = {
        "l0_qw": (1024, 256), "l0_kw": (1024, 256),
        "l0_q32w": (1024, 512), "l0_k32w": (1024, 512),
        "l0_vw": (1024, 1024), "l0_ow": (1024, 1024), "l0_g": (128, 1),
        "a1_qkv": (1024, 3072), "a1_ow": (1024, 1024), "a1_ob": (1, 1024),
        "m_gus": (1024, 4096), "m_ds": (2048, 1024),
        "m_gu": (8, 1024, 4096), "m_d": (8, 2048, 1024),
        "m_r": (1024, 8), "m_rb": (1, 8),
        "c_ident": (128, 128), "c_ones_row": (1, 512), "c_ones_col": (128, 1),
        "c_onesblk": (128, 2), "c_blk2": (2, 128), "c_esel": (8, 1024),
        "c_triu_mul": (128, 512), "c_tri_add": (128, 128), "c_lmask": (512, 1024),
        "c_mlo": (128, 128), "c_mhi": (128, 128),
        "c_rot_aq": (4096, 64), "c_rot_bq": (4096, 64),
        "c_rot_ak": (4096, 64), "c_rot_bk": (4096, 64),
    }
    for l in range(2):
        sh[f"f{l}_qkv"] = (1024, 1536)
        sh[f"f{l}_wo"] = (512, 1024)
        sh[f"f{l}_w1a"] = (1024, F_HIDP)
        sh[f"f{l}_w1g"] = (1024, F_HIDP)
        sh[f"f{l}_b1a"] = (1, F_HIDP)
        sh[f"f{l}_b1g"] = (1, F_HIDP)
        sh[f"f{l}_w2"] = (F_HIDP, 1024)
        sh[f"f{l}_b2"] = (1, 1024)
        sh[f"f{l}_lng"] = (128, 8)
        sh[f"f{l}_lnb"] = (128, 8)
        sh[f"f{l}_qs"] = (1, 512)
        sh[f"f{l}_ks"] = (1, 512)
    return sh


def _xchunk(dram_ap, c, width=CH):
    return dram_ap[:, c * width:(c + 1) * width].rearrange("(n p) t -> p n t", p=128)


def _sb3(sb_ap, k):
    return sb_ap.rearrange("p (k n) -> p k n", k=k)


def _wtiles(dram_ap):
    return dram_ap.rearrange("(k p) n -> p k n", p=128)



def plan_ln_chunk(P, sb, deps_x, affine=None, war_h=(), war_pm=(), war_bc=()):
    ctx, cs = P.ctx, P.cs
    xs_t, xq, h_t, rows = sb["xs_tiles"], sb["xq"], sb["h_tiles"], sb["rows"]
    pM, pB1, pB2 = sb["pM"], sb["pB1"], sb["pB2"]
    tok_sq = None
    for ct in range(8):
        tok_sq = ctx.s(lambda e, ct=ct: e.activation(
            out=xq[:, 512 * ct:512 * (ct + 1)], in_=xs_t[ct], func=AF.Square),
            deps=list(deps_x) + list(war_pm))
    mms0 = [dict(out=pM[0:1, :], lhsT=cs["ones_col_f"][:], rhs=xs_t[ct],
                 start=(ct == 0), stop=(ct == 7)) for ct in range(8)]
    tok_s0 = ctx.mm(mms0, deps=list(deps_x) + list(war_pm))
    mms1 = [dict(out=pM[32:33, :], lhsT=cs["ones_col_f"][:], rhs=xq[:, 512 * ct:512 * (ct + 1)],
                 start=(ct == 0), stop=(ct == 7)) for ct in range(8)]
    tok_s1 = ctx.mm(mms1, deps=[tok_sq])
    r_negm, r_m2, r_var, r_sd = (rows[:, 0:512], rows[:, 512:1024],
                                 rows[:, 1024:1536], rows[:, 1536:2048])
    tok_negm = ctx.v(lambda e: e.tensor_scalar_mul(r_negm, pM[0:1, :], -1.0 / 1024.0),
                     deps=[tok_s1])
    tok_m2 = ctx.s(lambda e: e.activation(out=r_m2, in_=pM[0:1, :], func=AF.Square,
                                          scale=1.0 / 1024.0), deps=[tok_s1, tok_negm])
    tok_var = ctx.v(lambda e: e.scalar_tensor_tensor(out=r_var, in0=pM[32:33, :],
                                                     scalar=1.0 / 1024.0, in1=r_m2,
                                                     op0=OP.mult, op1=OP.subtract),
                    deps=[tok_s1, tok_m2])
    tok_sd = ctx.s(lambda e: e.activation(out=r_sd, in_=r_var, func=AF.Sqrt,
                                          bias=P.eps_col[0:1, :]), deps=[tok_var])
    tok_rstd = ctx.v(lambda e: e.reciprocal(r_var, r_sd), deps=[tok_sd])
    tok_bn = ctx.mm([dict(out=pB1[:], lhsT=cs["ones_row_f"][0:1, 0:128], rhs=r_negm)],
                    deps=[tok_negm] + list(war_bc))
    tok_br = ctx.mm([dict(out=pB2[:], lhsT=cs["ones_row_f"][0:1, 0:128], rhs=r_var)],
                    deps=[tok_rstd] + list(war_bc))
    tok_h = None
    for ct in range(8):
        sl = slice(512 * ct, 512 * (ct + 1))
        ctx.v(lambda e, ct=ct, sl=sl: e.tensor_tensor(xq[:, sl], xs_t[ct], pB1[:], OP.add),
              deps=[tok_bn, tok_s1])
        if affine is None:
            tok_h = ctx.v(lambda e, ct=ct, sl=sl: e.tensor_tensor(h_t[ct], xq[:, sl],
                                                                  pB2[:], OP.mult),
                          deps=[tok_br] + list(war_h))
        else:
            lng, lnb = affine
            ctx.v(lambda e, ct=ct, sl=sl: e.tensor_tensor(xq[:, sl], xq[:, sl], pB2[:], OP.mult),
                  deps=[tok_br])
            tok_h = ctx.v(lambda e, ct=ct, sl=sl: e.tensor_scalar(
                out=h_t[ct], in0=xq[:, sl], scalar1=lng(ct), scalar2=lnb(ct),
                op0=OP.mult, op1=OP.add), deps=list(war_h))
    return tok_h, (tok_var, tok_m2), dict(tok_bn=tok_bn, tok_br=tok_br, tok_sq=tok_sq,
                                          tok_s1=tok_s1, tok_h=tok_h)



S1_LEVEL = 7


def plan_s1(P, xin, xout):
    lvl = S1_LEVEL
    nc, ctx, cs, dd = P.nc, P.ctx, P.cs, P.dram
    with ExitStack() as es:
        A = lambda name, shape, dt: es.enter_context(nc.sbuf_tensor("s1_" + name, shape, dt))
        w_q32 = A("wq32", [128, 4096], BF16)
        w_k32 = A("wk32", [128, 4096], BF16)
        w_qn = A("wqn", [128, 2048], BF16)
        w_kn = A("wkn", [128, 2048], BF16)
        w_v = A("wv", [128, 8192], BF16)
        w_o = A("wo", [128, 8192], BF16)
        g_col = A("gcol", [128, 1], F32)
        xs = A("xs", [128, 4096], F32)
        xq = A("xq", [128, 4096], F32)
        hT = A("hT", [128, 4096], BF16)
        rows = A("rows", [1, 2048], F32)
        rows2 = A("rows2", [2, 1024], F32)
        q32sb = A("q32", [128, 2048], BF16)
        k32sb = A("k32", [128, 2048], BF16)
        qn = A("qn", [128, 1024], F32)
        kn = A("kn", [128, 1024], F32)
        vN = A("vN", [128, 4096], BF16)
        qf = A("qf", [128, 1088], BF16)
        kf = A("kf", [128, 1088], BF16)
        qfT_l = A("qfTl", [16, 512], BF16)
        qfT_s = A("qfTs", [128, 1024], BF16)
        A_sb = A("Asb", [128, 2048], BF16)
        S_lin = A("Slin", [16, 1024], F32)
        S_sq = A("Ssq", [128, 2048], F32)
        S_0 = A("S0", [1, 1024], F32)
        Sb_lin = A("Sblin", [16, 1024], BF16)
        Sb_sq = A("Sbsq", [128, 2048], BF16)
        Sb_0 = A("Sb0", [1, 1024], BF16)
        ypair = A("ypair", [128, 4096], BF16)
        ysq = A("ysq", [128, 512], F32)
        ynorm = A("ynorm", [128, 4096], BF16)

        pf = [es.enter_context(nc.psum_tensor(f"s1_pf{i}", [128, 512], F32)) for i in range(4)]
        pM = es.enter_context(nc.psum_tensor("s1_pM", [128, 512], F32))
        pS = es.enter_context(nc.psum_tensor("s1_pS", [128, 512], F32))
        pt2 = [es.enter_context(nc.psum_tensor(f"s1_pt{i}", [128, 128], BF16)) for i in range(2)]

        if lvl < 2:
            tok_w = None
        else:
            tok_w = ctx.dma("gpsimd", "s1w", [
                (_sb3(w_q32, 8), _wtiles(dd["l0_q32w"])),
                (_sb3(w_k32, 8), _wtiles(dd["l0_k32w"])),
                (_sb3(w_qn, 8), _wtiles(dd["l0_qw"])),
                (_sb3(w_kn, 8), _wtiles(dd["l0_kw"])),
                (_sb3(w_v, 8), _wtiles(dd["l0_vw"])),
                (_sb3(w_o, 8), _wtiles(dd["l0_ow"])),
            ])
        tok_g = ctx.dma("sync", "s1g", [(g_col[:], dd["l0_g"])])

        sb = dict(xs_tiles=[xs[:, 512 * ct:512 * (ct + 1)] for ct in range(8)],
                  xq=xq, h_tiles=[hT[:, 512 * ct:512 * (ct + 1)] for ct in range(8)],
                  rows=rows, pM=pM, pB1=pf[2], pB2=pf[3])

        war = {}
        war["xs"] = []
        war["xq_store"] = None
        war["hT"] = []
        war["pM_rows"] = []
        war["pf"] = [None, None, None, None]
        war["qn"] = []
        war["q32"] = []
        war["vN"] = []
        war["qfkf"] = []
        war["qfT"] = []
        war["Asb"] = []
        war["ypair"] = [None] * 8
        war["ynorm"] = []
        war["pS"] = []
        war["pt2"] = [None, None]
        Sb_tok = [None] * 16
        Sy_tok = [None] * 16

        store_toks = []
        for c in range(NCH):
            tok_x = ctx.dma("sync", "xl", [(_sb3(xs[:], 8), _xchunk(xin, c))],
                            deps=war["xs"] + ([war["xq_store"]] if war["xq_store"] else []))
            war["xs"] = []
            if lvl < 1:
                tok_cp = ctx.v(lambda e: e.tensor_copy(xq[:], xs[:]),
                               deps=[tok_x, war["xq_store"]])
                tok_store = ctx.dma("sync", "xs_st", [(_xchunk(xout, c), _sb3(xq[:], 8))],
                                    deps=[tok_cp])
                war["xq_store"] = tok_store
                war["xs"].append(tok_cp)
                store_toks.append(tok_store)
                continue
            tok_h, tok_stats, _ln = plan_ln_chunk(
                P, sb, deps_x=[tok_x, war["xq_store"]],
                war_h=war["hT"], war_pm=war["pM_rows"],
                war_bc=[war["pf"][2], war["pf"][3]])
            war["xs"] += [tok_h, _ln["tok_s1"]]
            war["pM_rows"] = [tok_stats[0], tok_stats[1]]
            war["pf"][2] = tok_h
            war["pf"][3] = tok_h
            war["hT"] = []
            war["xq_store"] = None

            if lvl < 2:
                tok_store = ctx.dma("sync", "xs_st", [(_xchunk(xout, c), _sb3(xq[:], 8))],
                                    deps=[tok_h])
                war["xq_store"] = tok_store
                war["xs"].append(tok_h)
                store_toks.append(tok_store)
                continue
            pf_i = 0
            ev = {}
            def proj(lhsT_fn, rhs_fn, nK, out_slice, dst_ap, dst_dtype_scalar=True,
                     extra_deps=()):
                nonlocal pf_i
                b = pf_i % 2
                pf_i += 1
                mms = [dict(out=pf[b][out_slice[0], out_slice[1]], lhsT=lhsT_fn(k),
                            rhs=rhs_fn(k), start=(k == 0), stop=(k == nK - 1))
                       for k in range(nK)]
                tok_mm = ctx.mm(mms, deps=[tok_h, tok_w, war["pf"][b]] + list(extra_deps))
                tok_ev = ctx.s(lambda e: e.copy(out=dst_ap, in_=pf[b][out_slice[0], out_slice[1]]),
                               deps=[tok_mm])
                war["pf"][b] = tok_ev
                war["hT"].append(tok_mm)
                return tok_ev
            tok_q32 = tok_k32 = None
            for mt in range(4):
                tok_q32 = proj(lambda k, mt=mt: w_q32[:, 512 * k + 128 * mt:512 * k + 128 * mt + 128],
                               lambda k: hT[:, 512 * k:512 * (k + 1)], 8,
                               (slice(0, 128), slice(0, 512)), q32sb[:, 512 * mt:512 * (mt + 1)],
                               extra_deps=war["q32"])
                tok_k32 = proj(lambda k, mt=mt: w_k32[:, 512 * k + 128 * mt:512 * k + 128 * mt + 128],
                               lambda k: hT[:, 512 * k:512 * (k + 1)], 8,
                               (slice(0, 128), slice(0, 512)), k32sb[:, 512 * mt:512 * (mt + 1)],
                               extra_deps=war["q32"])
            war["q32"] = []
            tok_qn = tok_kn = None
            for tt in range(4):
                tok_qn = proj(lambda k, tt=tt: hT[:, 512 * k + 128 * tt:512 * k + 128 * tt + 128],
                              lambda k: w_qn[:, 256 * k:256 * (k + 1)], 8,
                              (slice(0, 128), slice(0, 256)), qn[:, 256 * tt:256 * (tt + 1)],
                              extra_deps=war["qn"])
                tok_kn = proj(lambda k, tt=tt: hT[:, 512 * k + 128 * tt:512 * k + 128 * tt + 128],
                              lambda k: w_kn[:, 256 * k:256 * (k + 1)], 8,
                              (slice(0, 128), slice(0, 256)), kn[:, 256 * tt:256 * (tt + 1)],
                              extra_deps=war["qn"])
            war["qn"] = []
            tok_v = None
            for tt in range(4):
                for nh in range(2):
                    tok_v = proj(lambda k, tt=tt: hT[:, 512 * k + 128 * tt:512 * k + 128 * tt + 128],
                                 lambda k, nh=nh: w_v[:, 1024 * k + 512 * nh:1024 * k + 512 * (nh + 1)],
                                 8, (slice(0, 128), slice(0, 512)),
                                 vN[:, 1024 * tt + 512 * nh:1024 * tt + 512 * (nh + 1)],
                                 extra_deps=war["vN"])
            war["vN"] = []

            if lvl < 3:
                tok_store = ctx.dma("sync", "xs_st", [(_xchunk(xout, c), _sb3(xq[:], 8))],
                                    deps=[tok_h, tok_v])
                war["xq_store"] = tok_store
                war["xs"].append(tok_h)
                store_toks.append(tok_store)
                continue
            tok_yn_pair = [None] * 8
            war_ysq = war.get("ysq")
            war_rows2 = war.get("rows2", (None, None))
            for h in range(L0_H):
                g = h % 4
                mt = h // 4
                build_toks = []
                if c > 0:
                    t1 = ctx.v(lambda e, h=h: e.tensor_scalar_mul(
                        qf.rearrange("p (tt f) -> p tt f", tt=4)[:, :, 0:16],
                        qn.rearrange("p (tt f) -> p tt f", tt=4)[:, :, 16 * h:16 * h + 16],
                        1.0 / RRD), deps=[tok_qn] + war["qfkf"])
                    build_toks.append(t1)
                    for tt in range(4):
                        sl = qn[:, 256 * tt + 16 * h:256 * tt + 16 * h + 16]
                        t2 = ctx.v(lambda e, tt=tt, sl=sl: e.scalar_tensor_tensor(
                            out=qf[:, 272 * tt + 16:272 * (tt + 1)].rearrange(
                                "p (i j) -> p i j", i=16),
                            in0=sl[:, :, None].broadcast_to([128, 16, 16]), scalar=SQS,
                            in1=sl[:, None, :].broadcast_to([128, 16, 16]),
                            op0=OP.mult, op1=OP.mult), deps=[tok_qn] + war["qfkf"])
                        build_toks.append(t2)
                t1 = ctx.v(lambda e, h=h: e.tensor_scalar_mul(
                    kf.rearrange("p (tt f) -> p tt f", tt=4)[:, :, 0:16],
                    kn.rearrange("p (tt f) -> p tt f", tt=4)[:, :, 16 * h:16 * h + 16],
                    1.0 / RRD), deps=[tok_kn] + war["qfkf"])
                build_toks.append(t1)
                for tt in range(4):
                    sl = kn[:, 256 * tt + 16 * h:256 * tt + 16 * h + 16]
                    t2 = ctx.v(lambda e, tt=tt, sl=sl: e.scalar_tensor_tensor(
                        out=kf[:, 272 * tt + 16:272 * (tt + 1)].rearrange(
                            "p (i j) -> p i j", i=16),
                        in0=sl[:, :, None].broadcast_to([128, 16, 16]), scalar=SQS,
                        in1=sl[:, None, :].broadcast_to([128, 16, 16]),
                        op0=OP.mult, op1=OP.mult), deps=[tok_kn] + war["qfkf"])
                    build_toks.append(t2)
                war["qfkf"] = []
                war["qn"] += build_toks

                tr_toks = []
                if c > 0:
                    jobs = []
                    for tt in range(4):
                        jobs.append((qf[:, 272 * tt:272 * tt + 16], 16,
                                     qfT_l[:, 128 * tt:128 * (tt + 1)]))
                        jobs.append((qf[:, 272 * tt + 16:272 * tt + 144], 128,
                                     qfT_s[:, 128 * tt:128 * tt + 128]))
                        jobs.append((qf[:, 272 * tt + 144:272 * tt + 272], 128,
                                     qfT_s[:, 512 + 128 * tt:512 + 128 * tt + 128]))
                    for j, (src, rows_n, dst) in enumerate(jobs):
                        b = j % 2
                        tok_t = ctx.tr(pt2[b][0:rows_n, :], src, cs["ident"][:],
                                       deps=build_toks + [war["pt2"][b]] + war["qfT"])
                        tok_e = ctx.v(lambda e, b=b, rows_n=rows_n, dst=dst:
                                      e.tensor_copy(dst, pt2[b][0:rows_n, :]), deps=[tok_t])
                        war["pt2"][b] = tok_e
                        war["qfkf"].append(tok_t)
                        tr_toks.append(tok_e)
                    war["qfT"] = []

                if lvl < 4:
                    war["qfkf"] += build_toks
                    continue
                A_toks = []
                for st in range(4):
                    b = st % 2
                    N = 512 - 128 * st
                    tok_pt = ctx.mm([dict(
                        out=pf[b][:, 128 * st:512],
                        lhsT=k32sb[32 * g:32 * g + 16, 512 * mt + 128 * st:512 * mt + 128 * st + 128],
                        rhs=q32sb[32 * g:32 * g + 16, 512 * mt + 128 * st:512 * (mt + 1)],
                        tp=(32 * g, 0))],
                        deps=[tok_q32, tok_k32, war["pf"][b]])
                    tok_sqA = ctx.s(lambda e, b=b, st=st, N=N: e.activation(
                        out=A_sb[:, 512 * st:512 * st + N], in_=pf[b][:, 128 * st:512],
                        func=AF.Square, scale=A_SC, bias=P.ab_col),
                        deps=[tok_pt] + war["Asb"])
                    war["pf"][b] = tok_sqA
                    war["q32"].append(tok_pt)
                    tok_A = ctx.v(lambda e, st=st, N=N: e.scalar_tensor_tensor(
                        out=A_sb[:, 512 * st:512 * st + N], in0=A_sb[:, 512 * st:512 * st + N],
                        scalar=0.5, in1=cs["triu"][:, 0:N], op0=OP.add, op1=OP.mult),
                        deps=[tok_sqA])
                    A_toks.append(tok_A)
                war["Asb"] = []

                if lvl < 5:
                    continue
                mms = []
                for st in range(4):
                    N = 512 - 128 * st
                    mms.append(dict(out=pf[2][0:64, 128 * st:512],
                                    lhsT=vN[:, 1024 * st + 64 * h:1024 * st + 64 * h + 64],
                                    rhs=A_sb[:, 512 * st:512 * st + N],
                                    start=(st == 0), stop=False))
                if c > 0:
                    mms.append(dict(out=pf[2][0:64, :], lhsT=Sb_lin[0:16, 64 * h:64 * h + 64],
                                    rhs=qfT_l[:], start=False, stop=False))
                    mms.append(dict(out=pf[2][0:64, :], lhsT=Sb_sq[:, 64 * h:64 * h + 64],
                                    rhs=qfT_s[:, 0:512], start=False, stop=False))
                    mms.append(dict(out=pf[2][0:64, :], lhsT=Sb_sq[:, 1024 + 64 * h:1024 + 64 * h + 64],
                                    rhs=qfT_s[:, 512:1024], start=False, stop=False))
                    mms.append(dict(out=pf[2][0:64, :], lhsT=Sb_0[0:1, 64 * h:64 * h + 64],
                                    rhs=cs["ones_row_b"][:], start=False, stop=False))
                mms[-1]["stop"] = True
                tok_y = ctx.mm(mms, deps=A_toks + tr_toks + [tok_v, war["pf"][2],
                                                             Sb_tok[h]])
                Sy_tok[h] = tok_y
                war["Asb"].append(tok_y)
                war["qfT"].append(tok_y)
                war["vN"].append(tok_y)
                pr = h // 2
                tok_yev = ctx.v(lambda e, h=h, pr=pr: e.tensor_copy(
                    ypair[64 * (h % 2):64 * (h % 2) + 64, 512 * pr:512 * (pr + 1)],
                    pf[2][0:64, :]), deps=[tok_y, war["ypair"][pr]])
                war["pf"][2] = tok_yev

                if lvl < 6:
                    continue
                mmsd = []
                def _vsl(tt):
                    return vN[:, 1024 * tt + 64 * h:1024 * tt + 64 * h + 64]
                for tt in range(4):
                    mmsd.append(dict(out=pS[:, 0:64], lhsT=kf[:, 272 * tt + 16:272 * tt + 144],
                                     rhs=_vsl(tt), start=(tt == 0), stop=(tt == 3)))
                for tt in range(4):
                    mmsd.append(dict(out=pS[:, 64:128], lhsT=kf[:, 272 * tt + 144:272 * tt + 272],
                                     rhs=_vsl(tt), start=(tt == 0), stop=(tt == 3)))
                for tt in range(4):
                    mmsd.append(dict(out=pS[0:16, 128:192], lhsT=kf[:, 272 * tt:272 * tt + 16],
                                     rhs=_vsl(tt), start=(tt == 0), stop=(tt == 3)))
                for tt in range(4):
                    mmsd.append(dict(out=pS[0:1, 192:256], lhsT=cs["ones_col_b"][:],
                                     rhs=_vsl(tt), start=(tt == 0), stop=(tt == 3)))
                tok_sd_mm = ctx.mm(mmsd, deps=build_toks + [tok_v] + war["pS"])
                war["vN"].append(tok_sd_mm)
                war["qfkf"].append(tok_sd_mm)
                war["pS"] = []
                alu = OP.add if c > 0 else OP.bypass
                S_sq_view = S_sq.rearrange("p (s hh f) -> p s hh f", s=2, hh=16)[:, :, h, :]
                pS_sq_view = pS[:, 0:128].rearrange("p (s f) -> p s f", s=2)
                if c > 0:
                    t_a1 = ctx.v(lambda e, a=S_sq_view, d=pS_sq_view: e.tensor_tensor(
                        a, a, d, OP.add), deps=[tok_sd_mm])
                    t_a2 = ctx.v(lambda e, h=h: e.tensor_tensor(
                        S_lin[:, 64 * h:64 * h + 64], S_lin[:, 64 * h:64 * h + 64],
                        pS[0:16, 128:192], OP.add), deps=[tok_sd_mm])
                    t_a3 = ctx.v(lambda e, h=h: e.tensor_tensor(
                        S_0[:, 64 * h:64 * h + 64], S_0[:, 64 * h:64 * h + 64],
                        pS[0:1, 192:256], OP.add), deps=[tok_sd_mm])
                else:
                    t_a1 = ctx.v(lambda e, a=S_sq_view, d=pS_sq_view: e.tensor_copy(a, d),
                                 deps=[tok_sd_mm])
                    t_a2 = ctx.v(lambda e, h=h: e.tensor_copy(S_lin[:, 64 * h:64 * h + 64],
                                                              pS[0:16, 128:192]), deps=[tok_sd_mm])
                    t_a3 = ctx.v(lambda e, h=h: e.tensor_copy(S_0[:, 64 * h:64 * h + 64],
                                                              pS[0:1, 192:256]), deps=[tok_sd_mm])
                war["pS"] = [t_a1, t_a2, t_a3]
                if c < NCH - 1:
                    Sb_sq_view = Sb_sq.rearrange("p (s hh f) -> p s hh f", s=2, hh=16)[:, :, h, :]
                    t_c1 = ctx.v(lambda e, a=Sb_sq_view, d=S_sq_view: e.tensor_copy(a, d),
                                 deps=[t_a1, Sy_tok[h]])
                    t_c2 = ctx.v(lambda e, h=h: e.tensor_copy(Sb_lin[:, 64 * h:64 * h + 64],
                                                              S_lin[:, 64 * h:64 * h + 64]),
                                 deps=[t_a2, Sy_tok[h]])
                    t_c3 = ctx.v(lambda e, h=h: e.tensor_copy(Sb_0[:, 64 * h:64 * h + 64],
                                                              S_0[:, 64 * h:64 * h + 64]),
                                 deps=[t_a3, Sy_tok[h]])
                    Sb_tok[h] = t_c3

                if h % 2 == 1:
                    tok_ysq = ctx.s(lambda e, pr=pr: e.activation(
                        out=ysq[:], in_=ypair[:, 512 * pr:512 * (pr + 1)], func=AF.Square),
                        deps=[tok_yev, war_ysq])
                    tok_ssq = ctx.mm([dict(out=pM[64:66, :], lhsT=cs["onesblk"][:], rhs=ysq[:],
                                           start=True, stop=True)],
                                     deps=[tok_ysq] + war["pM_rows"])
                    war_ysq = tok_ssq
                    tok_sd2 = ctx.s(lambda e: e.activation(out=rows2[:, 0:512], in_=pM[64:66, :],
                                                           func=AF.Sqrt, scale=1.0 / 64.0,
                                                           bias=P.eps_col[0:2, :]),
                                    deps=[tok_ssq, war_rows2[0]])
                    war["pM_rows"] = [tok_sd2]
                    tok_rc = ctx.v(lambda e: e.reciprocal(rows2[:, 512:1024], rows2[:, 0:512]),
                                   deps=[tok_sd2, war_rows2[1]])
                    tok_bc = ctx.mm([dict(out=pf[3][:, :], lhsT=cs["blk2"][:],
                                          rhs=rows2[:, 512:1024], start=True, stop=True)],
                                    deps=[tok_rc, war["pf"][3]])
                    war_rows2 = (tok_rc, tok_bc)
                    tok_yn = ctx.v(lambda e, pr=pr: e.scalar_tensor_tensor(
                        out=ynorm[:, 512 * pr:512 * (pr + 1)],
                        in0=ypair[:, 512 * pr:512 * (pr + 1)], scalar=g_col[:],
                        in1=pf[3][:, :], op0=OP.mult, op1=OP.mult),
                        deps=[tok_bc, tok_g] + war["ynorm"])
                    war["pf"][3] = tok_yn
                    war["ypair"][pr] = tok_yn
                    tok_yn_pair[pr] = tok_yn
            war["ynorm"] = []

            if lvl < 7:
                deps7 = [tok_h, tok_v] + [t for t in tok_yn_pair if t]
                tok_store = ctx.dma("sync", "xs_st", [(_xchunk(xout, c), _sb3(xq[:], 8))],
                                    deps=deps7)
                war["xq_store"] = tok_store
                war["xs"].append(tok_h)
                store_toks.append(tok_store)
                continue
            res_toks = []
            for oc in range(8):
                b = oc % 2
                mms = [dict(out=pf[b][:, :], lhsT=w_o[:, 1024 * p + 128 * oc:1024 * p + 128 * (oc + 1)],
                            rhs=ynorm[:, 512 * p:512 * (p + 1)], start=(p == 0), stop=(p == 7))
                       for p in range(8)]
                tok_wo = ctx.mm(mms, deps=tok_yn_pair + [tok_w, war["pf"][b]])
                war["ynorm"].append(tok_wo)
                tok_res = ctx.v(lambda e, b=b, oc=oc: e.tensor_tensor(
                    xq[:, 512 * oc:512 * (oc + 1)], xs[:, 512 * oc:512 * (oc + 1)],
                    pf[b][:, :], OP.add), deps=[tok_wo])
                war["pf"][b] = tok_res
                war["xs"].append(tok_res)
                res_toks.append(tok_res)
            tok_store = ctx.dma("sync", "xs_st", [(_xchunk(xout, c), _sb3(xq[:], 8))], deps=res_toks)
            war["xq_store"] = tok_store
            war["ysq"] = war_ysq
            war["rows2"] = war_rows2
            store_toks.append(tok_store)
        P.store_toks += store_toks



LAUNCH_GROUPS = [("s1", "s2", "s3", "s4")]

_PROG_CACHE = {}


def _get_prog(stages):
    key = tuple(stages)
    if key not in _PROG_CACHE:
        P = Prog(key)
        nc = P.build()
        _PROG_CACHE[key] = (P, nc)
    return _PROG_CACHE[key]


def kernel(**inputs):
    w = stage_weights(inputs)
    cn = _consts()
    arrays = {**w, **cn}
    x = np.asarray(inputs["L0_attn_hidden_states"], dtype=np.float32)
    cur = [np.ascontiguousarray(x[c].T) for c in range(NCORES)]
    for stages in LAUNCH_GROUPS:
        P, nc = _get_prog(stages)
        in_maps = []
        for c in range(NCORES):
            m = {k: arrays[k] for k in P.input_keys}
            m["xin"] = cur[c]
            in_maps.append(m)
        res = run_bass_kernel_spmd(nc, in_maps, core_ids=list(range(NCORES)))
        cur = [np.asarray(res.results[c]["xout"]) for c in range(NCORES)]
    out = np.stack([np.ascontiguousarray(cur[c].T) for c in range(NCORES)])
    return np.ascontiguousarray(out).astype(np.float32)



S2_LAYERS = 2
S2_SKIP_FFN = False


def plan_s2(P, xin, xout):
    nc, ctx, cs, dd = P.nc, P.ctx, P.cs, P.dram
    zd = [nc.dram_tensor(f"s2_z{i}", [D, S], F32, kind="Internal").ap() for i in range(2)]

    with ExitStack() as es:
        A = lambda name, shape, dt: es.enter_context(nc.sbuf_tensor("s2_" + name, shape, dt))
        zin = A("zin", [128, 8 * 513], F32)
        zq = A("zq", [128, 4096], F32)
        rows = A("rows", [1, 2048], F32)
        with ExitStack() as pes:
            zf32 = pes.enter_context(nc.sbuf_tensor("s2_zf32", [128, 4096], F32))
            pM = pes.enter_context(nc.psum_tensor("s2p0_pM", [128, 512], F32))
            pB1 = pes.enter_context(nc.psum_tensor("s2p0_pB1", [128, 512], F32))
            pB2 = pes.enter_context(nc.psum_tensor("s2p0_pB2", [128, 512], F32))
            zin3 = zin.rearrange("p (n t) -> p n t", n=8)
            sb0 = dict(xs_tiles=[zin3[:, ct:ct + 1, 0:512].rearrange("p a t -> p (a t)")
                                 for ct in range(8)],
                       xq=zq,
                       h_tiles=[zf32[:, 512 * ct:512 * (ct + 1)] for ct in range(8)],
                       rows=rows, pM=pM, pB1=pB1, pB2=pB2)
            war = {"zin": [], "hT": [], "pM": [], "pB": [None, None]}
            for c in range(NCH):
                tok_x = ctx.dma("sync", "zl", [(zin3[:, :, 0:512], _xchunk(xin, c))],
                                deps=war["zin"])
                tok_h, tok_stats, _ln = plan_ln_chunk(
                    P, sb0, deps_x=[tok_x], war_h=war["hT"],
                    war_pm=war["pM"], war_bc=war["pB"])
                war["pM"] = [tok_stats[0], tok_stats[1]]
                war["pB"] = [tok_h, tok_h]
                war["zin"] = [tok_h, _ln["tok_s1"]]
                tok_st = ctx.dma("sync", "zst", [(_xchunk(zd[0], c), _sb3(zf32[:], 8))],
                                 deps=[tok_h])
                war["hT"] = [tok_st]

        z_ready = ("zst", ctx.cnt.get("zst", 0))
        P.barrier()

        cur = 0
        for l in range(S2_LAYERS):
            plan_s2_attn(P, es, l, zd[cur], zd[1 - cur], zin, zq, z_ready)
            z_ready = ("zst", ctx.cnt.get("zst", 0))
            cur = 1 - cur
            P.barrier()
            if S2_SKIP_FFN:
                continue
            plan_s2_ffn(P, es, l, zd[cur], zd[1 - cur], zin, zq, rows, z_ready)
            z_ready = ("zst", ctx.cnt.get("zst", 0))
            cur = 1 - cur
            P.barrier()

        zin3f = zin.rearrange("p (n t) -> p n t", n=8)
        war_x = []
        war_z2 = []
        prev_st = None
        for c in range(NCH):
            tok_x = ctx.dma("sync", "xl", [(zin3f[:, :, 0:512], _xchunk(xin, c))],
                            deps=war_x + [z_ready])
            tok_z = ctx.dma("sync", "zl", [(_sb3(zq[:], 8), _xchunk(zd[cur], c))],
                            deps=war_z2 + [z_ready, prev_st])
            tok_add = ctx.v(lambda e: e.tensor_tensor(
                _sb3(zq[:], 8), _sb3(zq[:], 8), zin3f[:, :, 0:512], OP.add),
                deps=[tok_x, tok_z, prev_st])
            tok_st = ctx.dma("sync", "xs_st", [(_xchunk(xout, c), _sb3(zq[:], 8))],
                             deps=[tok_add])
            war_x = [tok_add]
            war_z2 = [tok_add]
            prev_st = tok_st
            P.store_toks.append(tok_st)


def plan_s2_attn(P, es, l, zdin, zdout, zin, zq, z_ready):
    nc, ctx, cs, dd = P.nc, P.ctx, P.cs, P.dram
    zin3 = zin.rearrange("p (n t) -> p n t", n=8)
    with ExitStack() as pes:
        A = lambda name, shape, dt: pes.enter_context(
            nc.sbuf_tensor(f"s2a{l}_" + name, shape, dt))
        rotA = [A(f"rotA{i}", [128, 2048], F32) for i in range(2)]
        rotB = [A(f"rotB{i}", [128, 2048], F32) for i in range(2)]
        tok_rot = ctx.dma("sync", "s2rot", [
            (_sb3(rotA[0], 32), dd["c_rot_aq"].rearrange("(tt p) e -> p tt e", p=128)),
            (_sb3(rotB[0], 32), dd["c_rot_bq"].rearrange("(tt p) e -> p tt e", p=128)),
            (_sb3(rotA[1], 32), dd["c_rot_ak"].rearrange("(tt p) e -> p tt e", p=128)),
            (_sb3(rotB[1], 32), dd["c_rot_bk"].rearrange("(tt p) e -> p tt e", p=128)),
        ], deps=[z_ready])
        wqkv = A("wqkv", [128, 8 * 1536], BF16)
        wo = A("wo", [128, 4 * 1024], BF16)
        qsr = A("qsr", [1, 512], F32)
        ksr = A("ksr", [1, 512], F32)
        qsB = A("qsB", [128, 512], F32)
        ksB = A("ksB", [128, 512], F32)
        hs = A("hs", [128, 4096], BF16)
        qN = A("qN", [128, 2048], F32)
        kN = A("kN", [128, 2048], F32)
        sq = A("sq", [128, 2048], F32)
        n2 = A("n2", [128, 192], F32)
        qhr = A("qhr", [128, 2048], F32)
        qhT = A("qhT", [128, 2048], BF16)
        kjoin = A("kjoin", [128, 4096], BF16)
        vjoin = A("vjoin", [128, 4096], BF16)
        pI = A("pI", [128, 2560], BF16)
        pT = A("pT", [128, 2560], BF16)
        nmx = A("nmx", [128, 4], F32)
        rsum = A("rsum", [128, 4], F32)
        oT = A("oT", [128, 2048], BF16)
        pj = [pes.enter_context(nc.psum_tensor(f"s2a{l}_pj{i}", [128, 640], F32))
              for i in range(2)]
        po = [pes.enter_context(nc.psum_tensor(f"s2a{l}_po{i}", [128, 128], F32))
              for i in range(2)]
        ptr = [pes.enter_context(nc.psum_tensor(f"s2a{l}_pt{i}", [128, 128], BF16))
               for i in range(2)]

        def ptr_slot(s4):
            return ptr[s4 % 2][:, :]

        tok_w = ctx.dma("gpsimd", "s2w", [
            (_sb3(wqkv, 8), _wtiles(dd[f"f{l}_qkv"])),
            (_sb3(wo, 4), _wtiles(dd[f"f{l}_wo"])),
        ])
        tok_qs = ctx.dma("sync", "s2qs", [(qsr[:], dd[f"f{l}_qs"]),
                                          (ksr[:], dd[f"f{l}_ks"])])
        tok_qb = ctx.mm([dict(out=pj[0][:, 0:512], lhsT=cs["ones_row_f"][0:1, 0:128],
                              rhs=qsr[:])], deps=[tok_qs])
        tok_qb2 = ctx.v(lambda e: e.tensor_copy(qsB[:], pj[0][:, 0:512]), deps=[tok_qb])
        tok_kb = ctx.mm([dict(out=pj[1][:, 0:512], lhsT=cs["ones_row_f"][0:1, 0:128],
                              rhs=ksr[:])], deps=[tok_qs])
        tok_kb2 = ctx.v(lambda e: e.tensor_copy(ksB[:], pj[1][:, 0:512]), deps=[tok_kb])

        war = {"zin": [], "hs": [], "pj": [tok_qb2, tok_kb2], "po": [None, None],
               "ptr": [None] * 4, "qkN": [], "sq": [], "qhr": [],
               "qrb": [], "qhT": [], "kjoin": [], "vjoin": [], "oT": [],
               "n2": [],
               "nmx4": [None] * 4, "rsum4": [None] * 4,
               "pI4": [None] * 4, "pT4": [None] * 4,
               "zq": None}
        tmp2 = None
        pj_i = [0]

        def next_pj():
            b = pj_i[0] % 2
            pj_i[0] += 1
            return b

        for c in range(NCH):
            if c == 0:
                tok_ms = ctx.v(lambda e: e.memset(zin3[:, :, 0:1], 0.0), deps=war["zin"])
                tok_zl = ctx.dma("sync", "zl", [(zin3[:, :, 1:513], _xchunk(zdin, 0))],
                                 deps=war["zin"] + [z_ready])
                tok_zin = (tok_ms, tok_zl)
            else:
                src = zdin[:, c * 512 - 1:(c + 1) * 512].rearrange("(n p) t -> p n t", p=128)
                tok_zl = ctx.dma("sync", "zl", [(zin3[:, :, 0:513], src)],
                                 deps=war["zin"] + [z_ready])
                tok_zin = (tok_zl,)
            war["zin"] = []

            hs3 = hs.rearrange("p (n t) -> p n t", n=8)
            t_h1 = ctx.v(lambda e: e.tensor_copy(hs3[:, 0:4, :], zin3[:, 0:4, 1:513]),
                         deps=list(tok_zin) + war["hs"])
            t_h2 = ctx.v(lambda e: e.tensor_copy(hs3[:, 4:8, :], zin3[:, 4:8, 0:512]),
                         deps=list(tok_zin) + war["hs"])
            war["hs"] = []
            tok_hs = t_h2

            if c > 0:
                kj4 = kjoin.rearrange("p (n t) -> p n t", n=4)
                t_s1 = ctx.v(lambda e: e.tensor_copy(kj4[:, :, 0:512], kj4[:, :, 512:1024]),
                             deps=war["kjoin"])
                t_s2 = ctx.v(lambda e: e.tensor_copy(vjoin[:, 0:2048], vjoin[:, 2048:4096]),
                             deps=war["vjoin"])
                war["kjoin"] = [t_s1]
                war["vjoin"] = [t_s2]

            ev_toks = {}
            for tt in range(4):
                for part in range(3):
                    b = next_pj()
                    mms = [dict(out=pj[b][:, 0:512],
                                lhsT=hs[:, 512 * k + 128 * tt:512 * k + 128 * tt + 128],
                                rhs=wqkv[:, 1536 * k + 512 * part:1536 * k + 512 * (part + 1)],
                                start=(k == 0), stop=(k == 7)) for k in range(8)]
                    tok_mm = ctx.mm(mms, deps=[tok_hs, tok_w, war["pj"][b]])
                    if part == 0:
                        tok_ev = ctx.s(lambda e, b=b, tt=tt: e.copy(
                            out=qN[:, 512 * tt:512 * (tt + 1)], in_=pj[b][:, 0:512]),
                            deps=[tok_mm] + war["qkN"])
                    elif part == 1:
                        tok_ev = ctx.s(lambda e, b=b, tt=tt: e.copy(
                            out=kN[:, 512 * tt:512 * (tt + 1)], in_=pj[b][:, 0:512]),
                            deps=[tok_mm] + war["qkN"])
                    else:
                        tok_ev = ctx.s(lambda e, b=b, tt=tt: e.copy(
                            out=vjoin[:, 2048 + 512 * tt:2048 + 512 * (tt + 1)],
                            in_=pj[b][:, 0:512]),
                            deps=[tok_mm] + war["vjoin"])
                    war["pj"][b] = tok_ev
                    ev_toks[(tt, part)] = tok_ev
            war["qkN"] = []
            war["vjoin"] = [ev_toks[(3, 2)]]
            tok_qkN = ev_toks[(3, 1)]

            qrb = hs[:, 0:2048].rearrange("p t -> p t")
            krb = hs[:, 2048:4096].rearrange("p t -> p t")
            qrot_toks = []
            war_sq = war["sq"]
            for qk in range(2):
                xN = qN if qk == 0 else kN
                xout = qrb if qk == 0 else krb
                sc_row = qsB if qk == 0 else ksB
                sc_f = 8.0 if qk == 0 else 1.0
                base = 96 * qk
                t_sq = ctx.s(lambda e, xN=xN: e.activation(out=sq[:], in_=xN[:],
                                                           func=AF.Square),
                             deps=[tok_qkN] + war_sq)
                t_rd = ctx.v(lambda e, base=base: e.reduce_sum(
                    n2[:, base:base + 32], sq.rearrange("p (g e) -> p g e", e=64),
                    axis=AX.X), deps=[t_sq] + war["n2"])
                t_sd = ctx.s(lambda e, base=base: e.activation(
                    out=n2[:, base + 32:base + 64], in_=n2[:, base:base + 32],
                    func=AF.Sqrt), deps=[t_rd] + war["n2"])
                t_mx = ctx.v(lambda e, base=base: e.tensor_scalar_max(
                    n2[:, base + 32:base + 64], n2[:, base + 32:base + 64], 1e-12),
                    deps=[t_sd])
                t_rc = ctx.v(lambda e, base=base: e.reciprocal(
                    n2[:, base + 64:base + 96], n2[:, base + 32:base + 64]), deps=[t_mx])
                rn_view = n2[:, base + 64:base + 96, None].broadcast_to([128, 32, 64])
                t_q1 = ctx.v(lambda e, xN=xN, rn_view=rn_view: e.tensor_tensor(
                    qhr.rearrange("p (g e) -> p g e", e=64),
                    xN.rearrange("p (g e) -> p g e", e=64), rn_view, OP.mult),
                    deps=[t_rc] + war["qhr"])
                scb = sc_row[:, None, :].broadcast_to([128, 4, 512])
                tmp2 = xN
                t_q2 = ctx.v(lambda e, scb=scb, sc_f=sc_f, tmp2=tmp2: e.scalar_tensor_tensor(
                    out=tmp2.rearrange("p (g t) -> p g t", g=4),
                    in0=qhr.rearrange("p (g t) -> p g t", g=4), scalar=sc_f,
                    in1=scb, op0=OP.mult, op1=OP.mult),
                    deps=[t_q1, tok_qb2, tok_kb2])
                rA, rB = rotA[qk], rotB[qk]
                for tt in range(4):
                    ofs = 512 * tt
                    tofs = 64 * (4 * c + tt)
                    Av = rA[:, tofs:tofs + 64][:, None, :].broadcast_to([128, 8, 64])
                    Bv = rB[:, tofs:tofs + 64]
                    t_r1 = ctx.v(lambda e, ofs=ofs, Av=Av, tmp2=tmp2: e.tensor_tensor(
                        qhr[:, ofs:ofs + 512].rearrange("p (h e) -> p h e", e=64),
                        tmp2[:, ofs:ofs + 512].rearrange("p (h e) -> p h e", e=64),
                        Av, OP.mult), deps=[t_q2, tok_rot])
                    B1 = Bv[:, 0:32][:, None, :].broadcast_to([128, 8, 32])
                    B2 = Bv[:, 32:64][:, None, :].broadcast_to([128, 8, 32])
                    t_r2 = ctx.v(lambda e, ofs=ofs, B1=B1, tmp2=tmp2: e.tensor_tensor(
                        sq[:, ofs:ofs + 512].rearrange("p (h e) -> p h e", e=64)[:, :, 0:32],
                        tmp2[:, ofs:ofs + 512].rearrange("p (h e) -> p h e", e=64)[:, :, 32:64],
                        B1, OP.mult), deps=[t_q2, tok_rot])
                    t_r3 = ctx.v(lambda e, ofs=ofs, B2=B2, tmp2=tmp2: e.tensor_tensor(
                        sq[:, ofs:ofs + 512].rearrange("p (h e) -> p h e", e=64)[:, :, 32:64],
                        tmp2[:, ofs:ofs + 512].rearrange("p (h e) -> p h e", e=64)[:, :, 0:32],
                        B2, OP.mult), deps=[t_q2, tok_rot])
                    t_r4 = ctx.v(lambda e, ofs=ofs, xout=xout: e.tensor_tensor(
                        xout[:, ofs:ofs + 512], qhr[:, ofs:ofs + 512], sq[:, ofs:ofs + 512],
                        OP.add), deps=[t_r1, t_r2, t_r3, ev_toks[(3, 2)]] + war["qrb"])
                    qrot_toks.append(t_r4)
                war_sq = qrot_toks[-4:]
            war["qhr"] = []
            war["qrb"] = []
            war["n2"] = [t_q1]
            tok_rotdone = qrot_toks[-1]
            tok_shift_k = war["kjoin"]
            tok_shift_v = war["vjoin"]

            trb = [0]
            tr_pe_last = [None]
            tok_qk_last = [None]
            trq_tok = {}
            trk_tok = {}

            def emit_trans(h):
                pb = 64 * (h % 2)
                pr = h // 2
                for qk in range(2):
                    srcb = qrb if qk == 0 else krb
                    for tt in range(4):
                        b = trb[0] % 2
                        trb[0] += 1
                        psl = ptr_slot(b)
                        extra = war["qhT"] if qk == 0 else (war["kjoin"] + tok_shift_k)
                        t_t = ctx.tr(psl[0:64, :],
                                     srcb[:, 512 * tt + 64 * h:512 * tt + 64 * h + 64],
                                     cs["ident"][:],
                                     deps=[qrot_toks[4 * qk + tt], war["ptr"][b]] + extra)
                        if qk == 0:
                            dst = qhT[pb:pb + 64, 512 * pr + 128 * tt:512 * pr + 128 * tt + 128]
                        else:
                            dst = kjoin[pb:pb + 64,
                                        1024 * pr + 512 + 128 * tt:1024 * pr + 512 + 128 * tt + 128]
                        if trb[0] % 2 == 0:
                            t_e = ctx.v(lambda e, psl=psl, dst=dst: e.tensor_copy(
                                dst, psl[0:64, :]), deps=[t_t])
                        else:
                            t_e = ctx.s(lambda e, psl=psl, dst=dst: e.copy(
                                out=dst, in_=psl[0:64, :]), deps=[t_t])
                        war["ptr"][b] = t_e
                        tr_pe_last[0] = t_t
                        (trq_tok if qk == 0 else trk_tok)[(h, tt)] = t_e

            st_w = [None] * 32
            oevac_toks = []

            def emit_qk_soft(idx):
                h, it = divmod(idx, 4)
                pb = 64 * (h % 2)
                pr = h // 2
                jts = list(range(it, it + 5)) if c > 0 else list(range(4, it + 5))
                width = 128 * len(jts)
                b = next_pj()
                n0 = min(width, 512)
                mms = [dict(out=pj[b][:, 0:n0],
                            lhsT=qhT[pb:pb + 64, 512 * pr + 128 * it:512 * pr + 128 * it + 128],
                            rhs=kjoin[pb:pb + 64,
                                      1024 * pr + 128 * jts[0]:1024 * pr + 128 * jts[0] + n0],
                            start=True, stop=True)]
                if width > 512:
                    mms.append(dict(out=pj[b][:, 512:width],
                                    lhsT=qhT[pb:pb + 64, 512 * pr + 128 * it:512 * pr + 128 * it + 128],
                                    rhs=kjoin[pb:pb + 64,
                                              1024 * pr + 128 * jts[0] + 512:1024 * pr + 128 * jts[0] + width],
                                    start=True, stop=True))
                qk_deps = ([trq_tok[(h, it)]] + [trk_tok[(h, tt)] for tt in range(4)]
                           + tok_shift_k + [war["pj"][b]])
                tok_qk = ctx.mm(mms, deps=qk_deps)
                tok_qk_last[0] = tok_qk
                mask_toks = []
                if c > 0:
                    mask_toks.append(ctx.v(lambda e, b=b: e.tensor_tensor(
                        pj[b][:, 0:128], pj[b][:, 0:128], cs["mlo"][:], OP.add),
                        deps=[tok_qk]))
                mask_toks.append(ctx.v(lambda e, b=b, width=width: e.tensor_tensor(
                    pj[b][:, width - 128:width], pj[b][:, width - 128:width],
                    cs["mhi"][:], OP.add), deps=[tok_qk]))
                t_mx = ctx.v(lambda e, b=b, it=it, width=width: e.reduce_max(
                    nmx[:, it:it + 1], pj[b][:, 0:width], axis=AX.X,
                    negate=True), deps=mask_toks + [war["nmx4"][it]])
                t_ex = ctx.s(lambda e, b=b, it=it, width=width: e.activation(
                    out=pI[:, 640 * it:640 * it + width], in_=pj[b][:, 0:width],
                    func=AF.Exp, bias=nmx[:, it:it + 1],
                    accum_out=rsum[:, it:it + 1]),
                    deps=[t_mx, war["rsum4"][it], war["pI4"][it]])
                war["pj"][b] = t_ex
                war["nmx4"][it] = t_ex
                st_w[idx] = [h, it, jts, width, t_ex, None]

            def emit_scale(idx):
                h, it, jts, width, t_ex, _ = st_w[idx]
                t_rc = ctx.v(lambda e, it=it: e.reciprocal(rsum[:, it:it + 1],
                                                           rsum[:, it:it + 1]),
                             deps=[t_ex])
                t_sc = ctx.v(lambda e, it=it, width=width: e.tensor_scalar_mul(
                    pI[:, 640 * it:640 * it + width], pI[:, 640 * it:640 * it + width],
                    rsum[:, it:it + 1]), deps=[t_rc])
                war["rsum4"][it] = t_sc
                st_w[idx][5] = t_sc

            def emit_pv(idx):
                h, it, jts, width, _, t_sc = st_w[idx]
                pb = 64 * (h % 2)
                trp_evac = []
                t_t = None
                for u in range(len(jts)):
                    b2 = trb[0] % 2
                    trb[0] += 1
                    psl = ptr_slot(b2)
                    t_t = ctx.tr(psl,
                                 pI[:, 640 * it + 128 * u:640 * it + 128 * (u + 1)],
                                 cs["ident"][:],
                                 deps=[t_sc, war["ptr"][b2], war["pT4"][it]])
                    dst = pT[:, 640 * it + 128 * u:640 * it + 128 * (u + 1)]
                    if trb[0] % 2 == 0:
                        t_e = ctx.v(lambda e, psl=psl, dst=dst: e.tensor_copy(
                            dst, psl), deps=[t_t])
                    else:
                        t_e = ctx.s(lambda e, psl=psl, dst=dst: e.copy(
                            out=dst, in_=psl), deps=[t_t])
                    war["ptr"][b2] = t_e
                    trp_evac.append(t_e)
                war["pI4"][it] = t_t
                bo = idx % 2
                mmsv = [dict(out=po[bo][0:64, :],
                             lhsT=vjoin[:, 512 * jts[u] + 64 * h:512 * jts[u] + 64 * h + 64],
                             rhs=pT[:, 640 * it + 128 * u:640 * it + 128 * (u + 1)],
                             start=(u == 0), stop=(u == len(jts) - 1))
                        for u in range(len(jts))]
                tok_pv = ctx.mm(mmsv, deps=trp_evac + [war["po"][bo]]
                                + tok_shift_v + war["vjoin"])
                war["pT4"][it] = tok_pv
                t_oe = ctx.v(lambda e, h=h, it=it, bo=bo, pb=pb: e.tensor_copy(
                    oT[pb:pb + 64, 512 * (h // 2) + 128 * it:512 * (h // 2) + 128 * it + 128],
                    po[bo][0:64, :]), deps=[tok_pv] + war["oT"])
                war["po"][bo] = t_oe
                oevac_toks.append(t_oe)
                return tok_pv

            emit_trans(0)
            tok_pv = None
            for idx in range(32):
                h, it = divmod(idx, 4)
                if it == 0 and h + 1 < 8:
                    emit_trans(h + 1)
                emit_qk_soft(idx)
                if idx >= 1:
                    emit_scale(idx - 1)
                if idx >= 2:
                    tok_pv = emit_pv(idx - 2)
            emit_scale(31)
            tok_pv = emit_pv(30)
            tok_pv = emit_pv(31)
            war["oT"] = []

            res_toks = []
            wo_last = None
            for oc in range(8):
                b = next_pj()
                mms = [dict(out=pj[b][:, 0:512],
                            lhsT=wo[:, 1024 * p + 128 * oc:1024 * p + 128 * (oc + 1)],
                            rhs=oT[:, 512 * p:512 * (p + 1)], start=(p == 0), stop=(p == 3))
                       for p in range(4)]
                wo_last = ctx.mm(mms, deps=oevac_toks[-1:] + [tok_w, war["pj"][b]])
                zv = zin3[:, oc:oc + 1, 1:513].rearrange("p a t -> p (a t)")
                t_res = ctx.v(lambda e, b=b, oc=oc, zv=zv: e.tensor_tensor(
                    zq[:, 512 * oc:512 * (oc + 1)], zv, pj[b][:, 0:512], OP.add),
                    deps=[wo_last, war["zq"]])
                war["pj"][b] = t_res
                res_toks.append(t_res)
            tok_st = ctx.dma("sync", "zst", [(_xchunk(zdout, c), _sb3(zq[:], 8))],
                             deps=res_toks)
            war["zq"] = tok_st
            war["zin"] = res_toks + [tok_hs]
            war["qkN"] = [t_sq, tok_rotdone]
            war["hs"] = [tr_pe_last[0], tok_pv]
            war["sq"] = [tok_rotdone]
            war["qrb"] = [tr_pe_last[0]]
            war["kjoin"] = [tok_qk_last[0]]
            war["vjoin"] = [tok_pv]
            war["qhT"] = [tok_qk_last[0]]
            war["oT"] = [wo_last]
            oevac_toks = []


def plan_s2_ffn(P, es, l, zdin, zdout, zin, zq, rows, z_ready):
    nc, ctx, cs, dd = P.nc, P.ctx, P.cs, P.dram
    zin3 = zin.rearrange("p (n t) -> p n t", n=8)
    with ExitStack() as pes:
        A = lambda name, shape, dt: pes.enter_context(
            nc.sbuf_tensor(f"s2f{l}_" + name, shape, dt))
        wt = [A(f"wt{i}", [128, 2048], BF16) for i in range(2)]
        w2 = A("w2", [128, F_NCT * 1024], BF16)
        b1a = A("b1a", [1, F_HIDP], BF16)
        b1g = A("b1g", [1, F_HIDP], BF16)
        b2 = A("b2", [1, 1024], BF16)
        lng = A("lng", [128, 8], F32)
        lnb = A("lnb", [128, 8], F32)
        hT = A("hT", [128, 4096], BF16)
        gl = A("gl", [128, 512], BF16)
        dT = A("dT", [128, F_NCT * 512], BF16)
        pM = pes.enter_context(nc.psum_tensor(f"s2f{l}_pM", [128, 512], F32))
        pB1 = pes.enter_context(nc.psum_tensor(f"s2f{l}_pB1", [128, 512], F32))
        pB2 = pes.enter_context(nc.psum_tensor(f"s2f{l}_pB2", [128, 512], F32))
        ph = [pes.enter_context(nc.psum_tensor(f"s2f{l}_ph{i}", [128, 512], F32))
              for i in range(2)]
        po = [pes.enter_context(nc.psum_tensor(f"s2f{l}_po{i}", [128, 512], F32))
              for i in range(2)]

        tok_w = ctx.dma("gpsimd", "s2w", [
            (_sb3(w2, F_NCT), _wtiles(dd[f"f{l}_w2"])),
            (b1a[:], dd[f"f{l}_b1a"]),
            (b1g[:], dd[f"f{l}_b1g"]),
            (b2[:], dd[f"f{l}_b2"]),
        ])

        def load_wt(buf, ct, deps):
            pairs = [
                (wt[buf][:, 0:1024].rearrange("p (k n) -> p k n", k=8),
                 dd[f"f{l}_w1a"][:, 128 * ct:128 * (ct + 1)].rearrange(
                     "(k p) n -> p k n", p=128)),
                (wt[buf][:, 1024:2048].rearrange("p (k n) -> p k n", k=8),
                 dd[f"f{l}_w1g"][:, 128 * ct:128 * (ct + 1)].rearrange(
                     "(k p) n -> p k n", p=128)),
            ]
            return ctx.dma("gpsimd", f"s2w{buf}", pairs, deps=deps)
        tok_ln = ctx.dma("sync", "s2qs", [
            (lng[:], dd[f"f{l}_lng"]),
            (lnb[:], dd[f"f{l}_lnb"]),
        ])

        war = {"zin": [], "pM": [], "pB": [None, None], "hT": [], "ph": [None, None],
               "po": [None, None], "gl": None, "dT": [], "zq": None}
        for c in range(NCH):
            if c == 0:
                tok_ms = ctx.v(lambda e: e.memset(zin3[:, :, 0:1], 0.0), deps=war["zin"])
                tok_zl = ctx.dma("sync", "zl", [(zin3[:, :, 1:513], _xchunk(zdin, 0))],
                                 deps=war["zin"] + [z_ready])
                tok_zin = [tok_ms, tok_zl]
            else:
                src = zdin[:, c * 512 - 1:(c + 1) * 512].rearrange("(n p) t -> p n t", p=128)
                tok_zl = ctx.dma("sync", "zl", [(zin3[:, :, 0:513], src)],
                                 deps=war["zin"] + [z_ready])
                tok_zin = [tok_zl]
            war["zin"] = []

            xs_tiles = [zin3[:, ct:ct + 1, (1 if ct < 4 else 0):(513 if ct < 4 else 512)]
                        .rearrange("p a t -> p (a t)") for ct in range(8)]
            sb = dict(xs_tiles=xs_tiles, xq=zq,
                      h_tiles=[hT[:, 512 * ct:512 * (ct + 1)] for ct in range(8)],
                      rows=rows, pM=pM, pB1=pB1, pB2=pB2)
            tok_h, tok_stats, _ln = plan_ln_chunk(
                P, sb, deps_x=tok_zin + [tok_ln, war["zq"]], war_h=war["hT"],
                war_pm=war["pM"], war_bc=war["pB"],
                affine=(lambda ct: lng[:, ct:ct + 1], lambda ct: lnb[:, ct:ct + 1]))
            war["pM"] = [tok_stats[0], tok_stats[1]]
            war["pB"] = [tok_h, tok_h]
            war["zin"] += [tok_h, _ln["tok_s1"]]

            d_toks = []
            prev_d = war["gl"]
            wt_tok = war.get("wt_tok", [None, None])
            wt_cons = war.get("wt_cons", [None, None])
            for ct in range(F_NCT):
                buf = ct % 2
                wt_tok[buf] = load_wt(buf, ct, deps=[wt_cons[buf]])
                ba = ct % 2
                mmsa = [dict(out=ph[ba][:, :],
                             lhsT=wt[buf][:, 128 * k:128 * (k + 1)],
                             rhs=hT[:, 512 * k:512 * (k + 1)], start=(k == 0), stop=False)
                        for k in range(8)]
                mmsa.append(dict(out=ph[ba][:, :], lhsT=b1a[0:1, 128 * ct:128 * (ct + 1)],
                                 rhs=cs["ones_row_b"][:], start=False, stop=True))
                tok_a = ctx.mm(mmsa, deps=[tok_h, tok_w, war["ph"][ba], wt_tok[buf]])
                bg = 1 - ba
                mmsg = [dict(out=ph[bg][:, :],
                             lhsT=wt[buf][:, 1024 + 128 * k:1024 + 128 * (k + 1)],
                             rhs=hT[:, 512 * k:512 * (k + 1)], start=(k == 0), stop=False)
                        for k in range(8)]
                mmsg.append(dict(out=ph[bg][:, :], lhsT=b1g[0:1, 128 * ct:128 * (ct + 1)],
                                 rhs=cs["ones_row_b"][:], start=False, stop=True))
                tok_g = ctx.mm(mmsg, deps=[tok_h, tok_w, war["ph"][bg], wt_tok[buf]])
                wt_cons[buf] = tok_g
                tok_gelu = ctx.s(lambda e, bg=bg: e.activation(out=gl[:], in_=ph[bg][:, :],
                                                               func=AF.Gelu),
                                 deps=[tok_g, prev_d])
                tok_d = ctx.v(lambda e, ba=ba, ct=ct: e.tensor_tensor(
                    dT[:, 512 * ct:512 * (ct + 1)], ph[ba][:, :], gl[:], OP.mult),
                    deps=[tok_a, tok_gelu] + war["dT"])
                war["ph"][ba] = tok_d
                war["ph"][bg] = tok_d
                prev_d = tok_d
                d_toks.append(tok_d)
            war["gl"] = d_toks[-1]
            war["dT"] = []

            res_toks = []
            for oc in range(8):
                bo = oc % 2
                mms = [dict(out=po[bo][:, :],
                            lhsT=w2[:, 1024 * ct + 128 * oc:1024 * ct + 128 * (oc + 1)],
                            rhs=dT[:, 512 * ct:512 * (ct + 1)], start=(ct == 0), stop=False)
                       for ct in range(F_NCT)]
                mms.append(dict(out=po[bo][:, :], lhsT=b2[0:1, 128 * oc:128 * (oc + 1)],
                                rhs=cs["ones_row_b"][:], start=False, stop=True))
                tok_o = ctx.mm(mms, deps=d_toks[-1:] + [tok_w, war["po"][bo]])
                war["dT"].append(tok_o)
                zv = zin3[:, oc:oc + 1, 1:513].rearrange("p a t -> p (a t)")
                t_res = ctx.v(lambda e, bo=bo, oc=oc, zv=zv: e.tensor_tensor(
                    zq[:, 512 * oc:512 * (oc + 1)], zv, po[bo][:, :], OP.add),
                    deps=[tok_o, war["zq"]])
                war["po"][bo] = t_res
                war["zin"].append(t_res)
                res_toks.append(t_res)
            tok_st = ctx.dma("sync", "zst", [(_xchunk(zdout, c), _sb3(zq[:], 8))],
                             deps=res_toks)
            war["zq"] = tok_st
            war["hT"] = war["dT"][-1:]
            war["wt_tok"] = wt_tok
            war["wt_cons"] = wt_cons



def plan_s3(P, xin, xout):
    nc, ctx, cs, dd = P.nc, P.ctx, P.cs, P.dram
    qT_d = nc.dram_tensor("s3_qT", [D, S], BF16, kind="Internal").ap()
    kT_d = nc.dram_tensor("s3_kT", [D, S], BF16, kind="Internal").ap()
    vN_d = nc.dram_tensor("s3_vN", [S, D], BF16, kind="Internal").ap()
    oN_d = nc.dram_tensor("s3_oN", [S, D], BF16, kind="Internal").ap()

    with ExitStack() as pes:
        A = lambda n, s, d: pes.enter_context(nc.sbuf_tensor("s3a_" + n, s, d))
        wqkv = A("wqkv", [128, 8 * 3072], BF16)
        xs = A("xs", [128, 4096], F32)
        xq = A("xq", [128, 4096], F32)
        hT = A("hT", [128, 4096], BF16)
        rows = A("rows", [1, 2048], F32)
        qk_sb = A("qk", [128, 8192], BF16)
        v_sb = A("v", [128, 4096], BF16)
        pM = pes.enter_context(nc.psum_tensor("s3a_pM", [128, 512], F32))
        pB1 = pes.enter_context(nc.psum_tensor("s3a_pB1", [128, 512], F32))
        pB2 = pes.enter_context(nc.psum_tensor("s3a_pB2", [128, 512], F32))
        pf = [pes.enter_context(nc.psum_tensor(f"s3a_pf{i}", [128, 512], F32))
              for i in range(2)]

        tok_w = ctx.dma("gpsimd", "s3w", [(_sb3(wqkv, 8), _wtiles(dd["a1_qkv"]))])
        war = {"xs": [], "xq_store": None, "hT": [], "pM": [], "pB": [None, None],
               "pf": [None, None], "qk": [], "v": []}
        for c in range(NCH):
            tok_x = ctx.dma("sync", "xl", [(_sb3(xs[:], 8), _xchunk(xin, c))],
                            deps=war["xs"])
            war["xs"] = []
            sb = dict(xs_tiles=[xs[:, 512 * ct:512 * (ct + 1)] for ct in range(8)],
                      xq=xq, h_tiles=[hT[:, 512 * ct:512 * (ct + 1)] for ct in range(8)],
                      rows=rows, pM=pM, pB1=pB1, pB2=pB2)
            tok_h, tok_stats, _ln = plan_ln_chunk(
                P, sb, deps_x=[tok_x], war_h=war["hT"], war_pm=war["pM"],
                war_bc=war["pB"])
            war["pM"] = [tok_stats[0], tok_stats[1]]
            war["pB"] = [tok_h, tok_h]
            war["xs"] += [tok_h, _ln["tok_s1"]]
            ev = []
            for qk in range(2):
                for mt in range(8):
                    b = (qk * 8 + mt) % 2
                    mms = [dict(out=pf[b][:, 0:512],
                                lhsT=wqkv[:, 3072 * k + 1024 * qk + 128 * mt:
                                          3072 * k + 1024 * qk + 128 * (mt + 1)],
                                rhs=hT[:, 512 * k:512 * (k + 1)],
                                start=(k == 0), stop=(k == 7)) for k in range(8)]
                    tok_mm = ctx.mm(mms, deps=[tok_h, tok_w, war["pf"][b]])
                    tok_ev = ctx.s(lambda e, b=b, qk=qk, mt=mt, pf=pf: e.copy(
                        out=qk_sb[:, 4096 * qk + 512 * mt:4096 * qk + 512 * (mt + 1)],
                        in_=pf[b][:, 0:512]), deps=[tok_mm] + war["qk"])
                    war["pf"][b] = tok_ev
                    ev.append(tok_ev)
            war["qk"] = []
            for tt in range(4):
                for nh in range(2):
                    b = (tt * 2 + nh) % 2
                    mms = [dict(out=pf[b][:, 0:512],
                                lhsT=hT[:, 512 * k + 128 * tt:512 * k + 128 * tt + 128],
                                rhs=wqkv[:, 3072 * k + 2048 + 512 * nh:
                                          3072 * k + 2048 + 512 * (nh + 1)],
                                start=(k == 0), stop=(k == 7)) for k in range(8)]
                    tok_mm = ctx.mm(mms, deps=[tok_h, tok_w, war["pf"][b]])
                    tok_ev = ctx.s(lambda e, b=b, tt=tt, nh=nh, pf=pf: e.copy(
                        out=v_sb[:, 1024 * tt + 512 * nh:1024 * tt + 512 * (nh + 1)],
                        in_=pf[b][:, 0:512]), deps=[tok_mm] + war["v"])
                    war["pf"][b] = tok_ev
                    ev.append(tok_ev)
            war["v"] = []
            tok_stq = ctx.dma("sync", "s3q", [
                (qT_d[:, 512 * c:512 * (c + 1)].rearrange("(n p) t -> p n t", p=128),
                 _sb3(qk_sb[:, 0:4096], 8)),
                (kT_d[:, 512 * c:512 * (c + 1)].rearrange("(n p) t -> p n t", p=128),
                 _sb3(qk_sb[:, 4096:8192], 8)),
            ], deps=ev[:16][-1:])
            tok_stv = ctx.dma("sync", "s3v", [
                (vN_d[512 * c:512 * (c + 1), :].rearrange("(n p) t -> p n t", p=128),
                 _sb3(v_sb[:], 4)),
            ], deps=ev[-1:])
            war["qk"] = [tok_stq]
            war["v"] = [tok_stv]
            war["hT"] = [ev[-1]]
    P.barrier()
    qkv_done = [("s3q", ctx.cnt.get("s3q", 0)), ("s3v", ctx.cnt.get("s3v", 0))]

    with ExitStack() as pes:
        A = lambda n, s, d: pes.enter_context(nc.sbuf_tensor("s3b_" + n, s, d))
        qT_g = A("qTg", [128, 4096], BF16)
        kbuf = [A(f"k{i}", [128, 4096], BF16) for i in range(2)]
        vbuf = [A(f"v{i}", [128, 4096], BF16) for i in range(2)]
        srow = A("srow", [128, 4 * 4096], BF16)
        pTt = A("pT", [128, 4 * 4096], BF16)
        ncol = A("ncol", [128, 8], F32)
        po_acc = A("poacc", [128, 4 * 1024], F32)
        o_ev = A("oev", [128, 4 * 1024], BF16)
        ps = [pes.enter_context(nc.psum_tensor(f"s3b_ps{i}", [128, 512], F32))
              for i in range(2)]
        pt2 = [pes.enter_context(nc.psum_tensor(f"s3b_pt{i}", [128, 128], BF16))
               for i in range(2)]
        pv = [pes.enter_context(nc.psum_tensor(f"s3b_pv{i}", [128, 1024], F32))
              for i in range(2)]

        war = {"qT": [], "k": [None, None], "v": [None, None], "ps": [None, None],
               "pt": [None, None], "pv": [None, None], "srow": [], "pTt": [],
               "ncol": [None] * 8, "po": [], "oev": []}
        for g in range(8):
            tok_q = ctx.dma("sync", "s3q", [
                (_sb3(qT_g, 8),
                 qT_d[:, 512 * g:512 * (g + 1)].rearrange("(n p) t -> p n t", p=128))],
                deps=war["qT"] + qkv_done)
            war["qT"] = []
            s_toks = []
            for kvc in range(g + 1):
                kb = kvc % 2
                tok_k = ctx.dma("sync", f"s3k{kb}", [
                    (_sb3(kbuf[kb], 8),
                     kT_d[:, 512 * kvc:512 * (kvc + 1)].rearrange("(n p) t -> p n t", p=128))],
                    deps=[war["k"][kb]] + qkv_done)
                for j in range(4):
                    width = 512 if kvc < g else 128 * (j + 1)
                    b = (kvc * 4 + j) % 2
                    mms = [dict(out=ps[b][:, 0:width],
                                lhsT=qT_g[:, 512 * k + 128 * j:512 * k + 128 * (j + 1)],
                                rhs=kbuf[kb][:, 512 * k:512 * k + width],
                                start=(k == 0), stop=(k == 7)) for k in range(8)]
                    tok_mm = ctx.mm(mms, deps=[tok_q, tok_k, war["ps"][b]])
                    dst0 = srow[:, 4096 * j + 512 * kvc:4096 * j + 512 * kvc + width - 128]
                    dstD = srow[:, 4096 * j + 512 * kvc + width - 128:
                                4096 * j + 512 * kvc + width]
                    if kvc < g:
                        tok_cp = ctx.v(lambda e, b=b, dst0=dst0, width=width: e.tensor_copy(
                            dst0, ps[b][:, 0:width - 128]) if width > 128 else None,
                            deps=[tok_mm] + war["srow"]) if width > 128 else None
                        tok_cp2 = ctx.v(lambda e, b=b, dstD=dstD, width=width: e.tensor_copy(
                            dstD, ps[b][:, width - 128:width]),
                            deps=[tok_mm] + war["srow"])
                    else:
                        tok_cp = None
                        if width > 128:
                            tok_cp = ctx.v(lambda e, b=b, dst0=dst0, width=width:
                                           e.tensor_copy(dst0, ps[b][:, 0:width - 128]),
                                           deps=[tok_mm] + war["srow"])
                        tok_cp2 = ctx.v(lambda e, b=b, dstD=dstD, width=width:
                                        e.tensor_tensor(dstD, ps[b][:, width - 128:width],
                                                        cs["tri_add"][:], OP.add),
                                        deps=[tok_mm] + war["srow"])
                    war["ps"][b] = tok_cp2
                    s_toks.append(tok_cp2)
                war["k"][kb] = s_toks[-1]
            war["srow"] = []
            tr_last = {}
            exp_toks = []
            for j in range(4):
                qlen = 512 * g + 128 * (j + 1)
                t_mx = ctx.v(lambda e, j=j, qlen=qlen: e.reduce_max(
                    ncol[:, j:j + 1], srow[:, 4096 * j:4096 * j + qlen], axis=AX.X,
                    negate=True), deps=s_toks[-1:] + [war["ncol"][j]])
                t_sc = ctx.v(lambda e, j=j: e.tensor_scalar_mul(
                    ncol[:, j:j + 1], ncol[:, j:j + 1], 1.0 / 32.0), deps=[t_mx])
                t_ex = ctx.s(lambda e, j=j, qlen=qlen: e.activation(
                    out=srow[:, 4096 * j:4096 * j + qlen],
                    in_=srow[:, 4096 * j:4096 * j + qlen], func=AF.Exp,
                    scale=1.0 / 32.0, bias=ncol[:, j:j + 1],
                    accum_out=ncol[:, 4 + j:5 + j]), deps=[t_sc, war["ncol"][4 + j]])
                t_rc = ctx.v(lambda e, j=j: e.reciprocal(ncol[:, 4 + j:5 + j],
                                                         ncol[:, 4 + j:5 + j]),
                             deps=[t_ex])
                t_ps = ctx.v(lambda e, j=j, qlen=qlen: e.tensor_scalar_mul(
                    srow[:, 4096 * j:4096 * j + qlen], srow[:, 4096 * j:4096 * j + qlen],
                    ncol[:, 4 + j:5 + j]), deps=[t_rc])
                war["ncol"][j] = t_ex
                war["ncol"][4 + j] = t_ps
                exp_toks.append(t_ps)
                nb = qlen // 128
                for u in range(nb):
                    b2 = u % 2
                    t_t = ctx.tr(pt2[b2][:, :],
                                 srow[:, 4096 * j + 128 * u:4096 * j + 128 * (u + 1)],
                                 cs["ident"][:], deps=[t_ps, war["pt"][b2]] + war["pTt"])
                    dst = pTt[:, 4096 * j + 128 * u:4096 * j + 128 * (u + 1)]
                    t_e = ctx.v(lambda e, b2=b2, dst=dst: e.tensor_copy(dst, pt2[b2][:, :]),
                                deps=[t_t])
                    war["pt"][b2] = t_e
                    tr_last[j] = t_e
            war["pTt"] = []
            war["srow"] = [tr_last[3]]
            acc_toks = {j: [] for j in range(4)}
            for kvc in range(g + 1):
                vb = kvc % 2
                tok_v = ctx.dma("sync", f"s3vv{vb}", [
                    (_sb3(vbuf[vb], 4),
                     vN_d[512 * kvc:512 * (kvc + 1), :].rearrange("(n p) t -> p n t", p=128))],
                    deps=[war["v"][vb]] + qkv_done)
                for j in range(4):
                    qlen = 512 * g + 128 * (j + 1)
                    njb = min(4, max(0, qlen // 128 - 4 * kvc))
                    if njb <= 0:
                        continue
                    bpv = (kvc * 4 + j) % 2
                    mms = []
                    for nh in range(2):
                        for u in range(njb):
                            mms.append(dict(
                                out=pv[bpv][:, 512 * nh:512 * (nh + 1)],
                                lhsT=pTt[:, 4096 * j + 512 * kvc + 128 * u:
                                         4096 * j + 512 * kvc + 128 * (u + 1)],
                                rhs=vbuf[vb][:, 1024 * u + 512 * nh:1024 * u + 512 * (nh + 1)],
                                start=(u == 0), stop=(u == njb - 1)))
                    tok_pv = ctx.mm(mms, deps=[tok_v, tr_last[j], war["pv"][bpv]])
                    if kvc == 0:
                        t_acc = ctx.v(lambda e, j=j, bpv=bpv: e.tensor_copy(
                            po_acc[:, 1024 * j:1024 * (j + 1)], pv[bpv][:, :]),
                            deps=[tok_pv] + war["po"])
                    else:
                        t_acc = ctx.v(lambda e, j=j, bpv=bpv: e.tensor_tensor(
                            po_acc[:, 1024 * j:1024 * (j + 1)],
                            po_acc[:, 1024 * j:1024 * (j + 1)], pv[bpv][:, :], OP.add),
                            deps=[tok_pv])
                    war["pv"][bpv] = t_acc
                    acc_toks[j].append(t_acc)
                war["v"][vb] = [v for v in acc_toks.values() if v][-1][-1]
            war["po"] = []
            ev_toks = []
            for j in range(4):
                t_c = ctx.v(lambda e, j=j: e.tensor_copy(
                    o_ev[:, 1024 * j:1024 * (j + 1)], po_acc[:, 1024 * j:1024 * (j + 1)]),
                    deps=acc_toks[j][-1:] + war["oev"])
                ev_toks.append(t_c)
            war["oev"] = []
            tok_st = ctx.dma("sync", "s3o", [
                (oN_d[512 * g:512 * (g + 1), :].rearrange("(n p) t -> p n t", p=128),
                 _sb3(o_ev, 4))], deps=ev_toks[-1:])
            war["po"] = [ev_toks[-1]]
            war["oev"] = [tok_st]
            war["qT"] = [s_toks[-1]]
    P.barrier()
    o_done = [("s3o", ctx.cnt.get("s3o", 0))]

    with ExitStack() as pes:
        A = lambda n, s, d: pes.enter_context(nc.sbuf_tensor("s3c_" + n, s, d))
        ow = A("ow", [128, 8 * 1024], BF16)
        ob = A("ob", [1, 1024], BF16)
        oT = A("oT", [128, 4096], BF16)
        xs = A("xs", [128, 4096], F32)
        xo = A("xo", [128, 4096], F32)
        pf = [pes.enter_context(nc.psum_tensor(f"s3c_pf{i}", [128, 512], F32))
              for i in range(2)]
        tok_w = ctx.dma("gpsimd", "s3w", [(_sb3(ow, 8), _wtiles(dd["a1_ow"])),
                                          (ob[:], dd["a1_ob"])])
        war = {"xs": [], "oT": [], "pf": [None, None], "xo": None}
        for c in range(NCH):
            tok_x = ctx.dma("sync", "xl", [(_sb3(xs[:], 8), _xchunk(xin, c))],
                            deps=war["xs"])
            war["xs"] = []
            def tr_fn(e, s, c=c):
                for ct in range(8):
                    _inc(e.dma_start(
                        out=oT[:, 512 * ct:512 * (ct + 1)],
                        in_=oN_d[512 * c:512 * (c + 1), 128 * ct:128 * (ct + 1)],
                        transpose=True), s, 16)
            tok_o = ctx.op("sync", tr_fn, deps=war["oT"] + o_done, dma_n=8)
            war["oT"] = []
            res_toks = []
            for oc in range(8):
                b = oc % 2
                mms = [dict(out=pf[b][:, 0:512],
                            lhsT=ow[:, 1024 * k + 128 * oc:1024 * k + 128 * (oc + 1)],
                            rhs=oT[:, 512 * k:512 * (k + 1)], start=(k == 0), stop=False)
                       for k in range(8)]
                mms.append(dict(out=pf[b][:, 0:512], lhsT=ob[0:1, 128 * oc:128 * (oc + 1)],
                                rhs=cs["ones_row_b"][:], start=False, stop=True))
                tok_mm = ctx.mm(mms, deps=[tok_o, tok_w, war["pf"][b]])
                war["oT"].append(tok_mm)
                t_res = ctx.v(lambda e, b=b, oc=oc: e.tensor_tensor(
                    xo[:, 512 * oc:512 * (oc + 1)], xs[:, 512 * oc:512 * (oc + 1)],
                    pf[b][:, 0:512], OP.add), deps=[tok_mm, tok_x, war["xo"]])
                war["pf"][b] = t_res
                war["xs"].append(t_res)
                res_toks.append(t_res)
            tok_st = ctx.dma("sync", "xs_st", [(_xchunk(xout, c), _sb3(xo[:], 8))],
                             deps=res_toks)
            war["xo"] = tok_st
            P.store_toks.append(tok_st)



def plan_s4(P, xin, xout):
    nc, ctx, cs, dd = P.nc, P.ctx, P.cs, P.dram
    with ExitStack() as es:
        A = lambda n, s, d: es.enter_context(nc.sbuf_tensor("s4_" + n, s, d))
        zT = A("zT", [128, 8 * 2048], BF16)
        acc = A("acc", [128, 8 * 2048], F32)
        cwT = A("cwT", [8, 2048], BF16)
        sg = A("sg", [128, 512], BF16)
        wr_sb = A("wr", [128, 8 * 8], BF16)
        rb_sb = A("rb", [1, 8], BF16)
        probs = A("probs", [128, 16 * 8], F32)
        rsc = A("rsc", [128, 16 * 8], F32)
        rsb = A("rsb", [128, 16 * 8], BF16)
        pm1 = A("pm1", [128, 32], F32)
        wt = [A(f"wt{i}", [128, 4096], BF16) for i in range(2)]
        wd = [A(f"wd{i}", [128, 4096], BF16) for i in range(2)]
        xq = acc[:, 0:4096]
        tok_wr = ctx.dma("gpsimd", "s4wr", [
            (_sb3(wr_sb, 8), _wtiles(dd["m_r"])),
            (rb_sb[:], dd["m_rb"]),
        ])

        war = {"zT": [], "acc": [], "probs": None, "pm1": None, "rsc": None,
               "cwT": [], "dT": [], "sg": None, "xs": [], "xo": None,
               "wt": [None, None], "wd": [None, None]}

        for half in range(2):
            with ExitStack() as pes:
                xs = pes.enter_context(nc.sbuf_tensor(f"s4l{half}_xs", [128, 4096], F32))
                rows = pes.enter_context(nc.sbuf_tensor(f"s4l{half}_rows", [1, 2048], F32))
                pM = pes.enter_context(nc.psum_tensor(f"s4l{half}_pM", [128, 512], F32))
                pB1 = pes.enter_context(nc.psum_tensor(f"s4l{half}_pB1", [128, 512], F32))
                pB2 = pes.enter_context(nc.psum_tensor(f"s4l{half}_pB2", [128, 512], F32))
                war_l = {"xs": [], "pM": [], "pB": [None, None], "hT": war["zT"]}
                tok_h = None
                for cc in range(4):
                    c = 4 * half + cc
                    tok_x = ctx.dma("sync", "xl", [(_sb3(xs[:], 8), _xchunk(xin, c))],
                                    deps=war_l["xs"])
                    h_tiles = [zT[:, 2048 * ct + 512 * cc:2048 * ct + 512 * (cc + 1)]
                               for ct in range(8)]
                    sb = dict(xs_tiles=[xs[:, 512 * ct:512 * (ct + 1)] for ct in range(8)],
                              xq=xq, h_tiles=h_tiles, rows=rows, pM=pM, pB1=pB1, pB2=pB2)
                    tok_h, tok_stats, _ln = plan_ln_chunk(
                        P, sb, deps_x=[tok_x, war["xo"]], war_h=war_l["hT"],
                        war_pm=war_l["pM"], war_bc=war_l["pB"])
                    war_l["pM"] = [tok_stats[0], tok_stats[1]]
                    war_l["pB"] = [tok_h, tok_h]
                    war_l["xs"] = [tok_h, _ln["tok_s1"]]
                    war_l["hT"] = []
                tok_z = tok_h
            P.barrier()

            with ExitStack() as pes:
                pr = pes.enter_context(nc.psum_tensor(f"s4r{half}_pr", [128, 512], F32))
                ptp = pes.enter_context(nc.psum_tensor(f"s4r{half}_pt", [128, 128], BF16))
                tok_ev = war["rsc"]
                for tl in range(16):
                    mms = [dict(out=pr[:, 0:8],
                                lhsT=zT[:, 2048 * k + 128 * tl:2048 * k + 128 * (tl + 1)],
                                rhs=wr_sb[:, 8 * k:8 * (k + 1)],
                                start=(k == 0), stop=False) for k in range(8)]
                    mms.append(dict(out=pr[:, 0:8],
                                    lhsT=cs["ones_row_b"][0:1, 0:128],
                                    rhs=rb_sb[0:1, 0:8], start=False, stop=True))
                    tok_lg = ctx.mm(mms, deps=[tok_z, tok_wr, tok_ev])
                    tok_ev = ctx.s(lambda e, tl=tl, pr=pr: e.copy(
                        out=rsc[:, 8 * tl:8 * (tl + 1)], in_=pr[:, 0:8]), deps=[tok_lg])
                tok_sig = ctx.s(lambda e: e.activation(out=probs[:], in_=rsc[:],
                                                       func=AF.Sigmoid),
                                deps=[tok_ev, war["probs"]])
                p3 = probs.rearrange("p (t e) -> p t e", e=8)
                r3 = rsc.rearrange("p (t e) -> p t e", e=8)
                t_m1 = ctx.v(lambda e: e.reduce_max(pm1[:, 0:16], p3, axis=AX.X),
                             deps=[tok_sig, war["pm1"]])
                m1b = pm1[:, 0:16, None].broadcast_to([128, 16, 8])
                t_lt = ctx.v(lambda e: e.tensor_tensor(r3, p3, m1b, OP.is_lt),
                             deps=[t_m1, war["rsc"]])
                t_t2 = ctx.v(lambda e: e.tensor_tensor(r3, p3, r3, OP.mult), deps=[t_lt])
                t_m2 = ctx.v(lambda e: e.reduce_max(pm1[:, 16:32], r3, axis=AX.X),
                             deps=[t_t2])
                m2b = pm1[:, 16:32, None].broadcast_to([128, 16, 8])
                t_ge = ctx.v(lambda e: e.tensor_tensor(r3, p3, m2b, OP.is_ge),
                             deps=[t_m2])
                t_cw = ctx.v(lambda e: e.tensor_tensor(r3, p3, r3, OP.mult), deps=[t_ge])
                t_cwb = ctx.v(lambda e: e.tensor_copy(rsb[:], rsc[:]), deps=[t_cw])
                war["probs"] = t_cw
                war["pm1"] = t_ge
                tr_tok = None
                for tl in range(16):
                    t_t = ctx.tr(ptp[0:8, :], rsb[:, 8 * tl:8 * (tl + 1)], cs["ident"][:],
                                 deps=[t_cwb, tr_tok] + war["cwT"])
                    tr_tok = ctx.v(lambda e, tl=tl, ptp=ptp: e.tensor_copy(
                        cwT[:, 128 * tl:128 * (tl + 1)], ptp[0:8, :]), deps=[t_t])
                war["cwT"] = []
                war["rsc"] = t_cwb
                tok_cw = tr_tok
            P.barrier()

            with ExitStack() as pes:
                dT = pes.enter_context(nc.sbuf_tensor(f"s4e{half}_dT",
                                                      [128, 16 * 2048], BF16))
                pg = [pes.enter_context(nc.psum_tensor(f"s4e{half}_pg{i}", [128, 512], F32))
                      for i in range(2)]
                pu = [pes.enter_context(nc.psum_tensor(f"s4e{half}_pu{i}", [128, 512], F32))
                      for i in range(2)]
                pd = [pes.enter_context(nc.psum_tensor(f"s4e{half}_pd{i}", [128, 512], F32))
                      for i in range(2)]
                pcw = pes.enter_context(nc.psum_tensor(f"s4e{half}_pcw", [128, 512], F32))
                war_cwB = None
                for ei in range(9):
                    shared = (ei == 0)
                    e_idx = ei - 1
                    d_toks = []
                    prev_d = war["sg"]
                    for otp in range(8):
                        buf = otp % 2
                        if shared:
                            gsrc = dd["m_gus"][:, 512 * otp:512 * (otp + 1)].rearrange(
                                "(k p) n -> p k n", p=128)
                        else:
                            gsrc = dd["m_gu"][e_idx, :, 512 * otp:512 * (otp + 1)].rearrange(
                                "(k p) n -> p k n", p=128)
                        tok_l = ctx.dma("gpsimd", f"m4w{buf}", [
                            (_sb3(wt[buf], 8), gsrc)], deps=[war["wt"][buf]])
                        for oth in range(2):
                            ot = 2 * otp + oth
                            gofs = 512 * 0 + 256 * oth
                            for tc in range(4):
                                bg = tc % 2
                                mmsg = [dict(out=pg[bg][:, :],
                                             lhsT=wt[buf][:, 512 * k + 256 * oth:
                                                          512 * k + 256 * oth + 128],
                                             rhs=zT[:, 2048 * k + 512 * tc:2048 * k + 512 * (tc + 1)],
                                             start=(k == 0), stop=(k == 7)) for k in range(8)]
                                tok_g = ctx.mm(mmsg, deps=[tok_l, tok_z,
                                                           _pgwar(war, "pg", bg)])
                                mmsu = [dict(out=pu[bg][:, :],
                                             lhsT=wt[buf][:, 512 * k + 256 * oth + 128:
                                                          512 * k + 256 * oth + 256],
                                             rhs=zT[:, 2048 * k + 512 * tc:2048 * k + 512 * (tc + 1)],
                                             start=(k == 0), stop=(k == 7)) for k in range(8)]
                                tok_u = ctx.mm(mmsu, deps=[tok_l, _pgwar(war, "pu", bg)])
                                tok_sil = ctx.s(lambda e, bg=bg, pg=pg: e.activation(
                                    out=sg[:], in_=pg[bg][:, :], func=AF.Silu),
                                    deps=[tok_g, prev_d])
                                tok_d = ctx.v(lambda e, bg=bg, ot=ot, tc=tc, pu=pu, dT=dT:
                                              e.tensor_tensor(
                                    dT[:, 2048 * ot + 512 * tc:2048 * ot + 512 * (tc + 1)],
                                    pu[bg][:, :], sg[:], OP.mult),
                                    deps=[tok_u, tok_sil] + war["dT"])
                                war.setdefault("pgw", [None, None])[bg] = tok_sil
                                war.setdefault("puw", [None, None])[bg] = tok_d
                                prev_d = tok_d
                                d_toks.append(tok_d)
                        war["wt"][buf] = tok_u
                    war["sg"] = d_toks[-1]
                    war["dT"] = []
                    dscale_tok = d_toks[-1]
                    if not shared:
                        for tc in range(4):
                            tok_b = ctx.mm([dict(out=pcw[:, :],
                                                 lhsT=cs["esel"][:, 128 * e_idx:128 * (e_idx + 1)],
                                                 rhs=cwT[0:8, 512 * tc:512 * (tc + 1)],
                                                 start=True, stop=True)],
                                           deps=[tok_cw, war_cwB])
                            dv = dT.rearrange("p (o t) -> p o t", o=16)[
                                :, :, 512 * tc:512 * (tc + 1)]
                            cb = pcw[:, None, :].broadcast_to([128, 16, 512])
                            dscale_tok = ctx.v(lambda e, dv=dv, cb=cb: e.tensor_tensor(
                                dv, dv, cb, OP.mult), deps=[tok_b, d_toks[-1]])
                            war_cwB = dscale_tok
                    for oc in range(8):
                        bufd = (oc // 2) % 2
                        och = oc % 2
                        if och == 0:
                            if shared:
                                dsrc = dd["m_ds"][:, 128 * oc:128 * (oc + 2)].rearrange(
                                    "(k p) n -> p k n", p=128)
                            else:
                                dsrc = dd["m_d"][e_idx, :, 128 * oc:128 * (oc + 2)].rearrange(
                                    "(k p) n -> p k n", p=128)
                            tok_ld = ctx.dma("gpsimd", f"m4d{bufd}", [
                                (_sb3(wd[bufd], 16), dsrc)], deps=[war["wd"][bufd]])
                        for tc in range(4):
                            bo = tc % 2
                            mms = [dict(out=pd[bo][:, :],
                                        lhsT=wd[bufd][:, 256 * k + 128 * och:
                                                      256 * k + 128 * och + 128],
                                        rhs=dT[:, 2048 * k + 512 * tc:2048 * k + 512 * (tc + 1)],
                                        start=(k == 0), stop=(k == 15)) for k in range(16)]
                            tok_w2 = ctx.mm(mms, deps=[tok_ld, dscale_tok,
                                                       _pgwar(war, "pd", bo)])
                            asl = acc[:, 2048 * oc + 512 * tc:2048 * oc + 512 * (tc + 1)]
                            if shared:
                                t_acc = ctx.v(lambda e, bo=bo, asl=asl, pd=pd: e.tensor_copy(
                                    asl, pd[bo][:, :]), deps=[tok_w2] + war["acc"])
                            else:
                                t_acc = ctx.v(lambda e, bo=bo, asl=asl, pd=pd: e.tensor_tensor(
                                    asl, asl, pd[bo][:, :], OP.add), deps=[tok_w2])
                            war.setdefault("pdw", [None, None])[bo] = t_acc
                            war["dT"].append(tok_w2)
                        if och == 1:
                            war["wd"][bufd] = tok_w2
                    war["acc"] = []
                    last_acc = t_acc
                war["zT"] = [tok_w2, tok_lg]
            P.barrier()

            with ExitStack() as pes:
                xs = pes.enter_context(nc.sbuf_tensor(f"s4f{half}_xs", [128, 4096], F32))
                prev_st = war["xo"]
                war_xf = []
                for cc in range(4):
                    c = 4 * half + cc
                    tok_x = ctx.dma("sync", "xl", [(_sb3(xs[:], 8), _xchunk(xin, c))],
                                    deps=war_xf + [prev_st])
                    acv = acc.rearrange("p (o t) -> p o t", o=8)[:, :, 512 * cc:512 * (cc + 1)]
                    t_res = ctx.v(lambda e, acv=acv, xs=xs: e.tensor_tensor(
                        _sb3(xs[:], 8), _sb3(xs[:], 8), acv, OP.add),
                        deps=[tok_x, last_acc, prev_st])
                    tok_st = ctx.dma("sync", "xs_st", [(_xchunk(xout, c), _sb3(xs[:], 8))],
                                     deps=[t_res])
                    war_xf = [t_res]
                    war["acc"].append(t_res)
                    prev_st = tok_st
                    P.store_toks.append(tok_st)
                war["xo"] = prev_st
            P.barrier()


def _pgwar(war, key, b):
    lst = war.get(key + "w")
    return lst[b] if lst else None
